# version 07
# baseline (speedup 1.0000x reference)
import itertools
import contextlib
import numpy as np

import concourse.bass as bass
import concourse.mybir as mybir
from concourse.bass_utils import run_bass_kernel_spmd

F32 = mybir.dt.float32
BF16 = mybir.dt.bfloat16
AF = mybir.ActivationFunctionType
ALU = mybir.AluOpType
AXL = mybir.AxisListType

NCORES = 8
D = 1024
NH = 8
NEG = -1.0e30

ENGS = ("sync", "tensor", "vector", "scalar", "gpsimd")
SELFCHAIN = {"vector": "ve", "scalar": "sc", "gpsimd": "gp"}


class Prog:

    def __init__(self, nc, stack):
        self.nc = nc
        self.stack = stack
        self.sems = {}
        self.cnt = {}
        self.q = {e: [] for e in ENGS}
        self.seen = {e: {} for e in ENGS}
        self.res = {}
        self.t = 0

    def new_sem(self, name):
        self.sems[name] = self.stack.enter_context(self.nc.semaphore(name))
        self.cnt[name] = 0
        return name

    def wait(self, eng, sname, val):
        if val is None or val <= 0 or self.seen[eng].get(sname, -1) >= val:
            return
        self.seen[eng][sname] = val
        h = self.sems[sname]
        self.t += 1
        self.q[eng].append((self.t, lambda e, h=h, v=val: e.wait_ge(h, v)))

    def op(self, eng, fn, inc=None, inc_by=1, waits=()):
        own = SELFCHAIN.get(eng)
        if own is not None:
            self.wait(eng, own, self.cnt[own])
        for w in waits:
            if w is None:
                continue
            self.wait(eng, w[0], w[1])
        if own is not None and inc is None:
            inc = own
        if inc is not None:
            h = self.sems[inc]
            self.cnt[inc] += inc_by

            def run(e, fn=fn, h=h, b=inc_by):
                fn(e).then_inc(h, b)

            self.t += 1
            self.q[eng].append((self.t, run))
            return self.cnt[inc]
        self.t += 1
        self.q[eng].append((self.t, fn))
        return None

    def acquire(self, name):
        w = self.res.get(name, [])
        self.res[name] = []
        return w

    def release(self, name, sem, cntv):
        self.res.setdefault(name, []).append((sem, cntv))


def host_consts():
    signs = [-1.0, 1.0]
    tri = np.array(list(itertools.product(signs, repeat=3)), dtype=np.float32)
    tri = tri / np.linalg.norm(tri, axis=1, keepdims=True)
    dirs = tri[:NH]
    emat = np.zeros((24, 32), np.float32)
    for h in range(8):
        for i in range(3):
            emat[3 * h + i, 4 * h + i] = 1.0
            emat[3 * h + i, 4 * h + 3] = dirs[h, i]
    hmat = np.zeros((32, 8), np.float32)
    for h in range(8):
        hmat[4 * h:4 * h + 4, h] = 1.0
    ident = np.eye(128, dtype=np.float32)
    ones = np.ones((128, 128), np.float32)
    return emat, hmat, ident, ones


def build_program(nseq=4, T=2048, ne=8):
    TT = T // 128
    T4 = T // 512
    AVW = TT * 1024
    W3B = 4 * T
    nc = bass.Bass("TRN2", target_bir_lowering=False)

    x_h = nc.dram_tensor("x", [nseq, T, D], F32, kind="ExternalInput").ap()
    y_h = nc.dram_tensor("y", [nseq, T, D], F32, kind="ExternalOutput").ap()
    wq_h = nc.dram_tensor("wq", [24, D], F32, kind="ExternalInput").ap()
    wk_h = nc.dram_tensor("wk", [24, D], F32, kind="ExternalInput").ap()
    wvt_h = nc.dram_tensor("wvt", [8, 128, D], F32, kind="ExternalInput").ap()
    wot_h = nc.dram_tensor("wot", [8, 128, D], F32, kind="ExternalInput").ap()
    sc_h = nc.dram_tensor("scale", [1, 1], F32, kind="ExternalInput").ap()
    w12t_h = nc.dram_tensor("w12t", [ne, 32, 128, D], F32, kind="ExternalInput").ap()
    w3t_h = nc.dram_tensor("w3t", [ne, 16, 128, D], F32, kind="ExternalInput").ap()
    gat_h = nc.dram_tensor("gatet", [8, 128, 8], F32, kind="ExternalInput").ap()
    cEq_h = nc.dram_tensor("cEq", [24, 32], F32, kind="ExternalInput").ap()
    cEk_h = nc.dram_tensor("cEk", [24, 32], F32, kind="ExternalInput").ap()
    cHm_h = nc.dram_tensor("cHm", [32, 8], F32, kind="ExternalInput").ap()
    cId_h = nc.dram_tensor("cId", [128, 128], F32, kind="ExternalInput").ap()
    cOn_h = nc.dram_tensor("cOn", [128, 128], F32, kind="ExternalInput").ap()

    with contextlib.ExitStack() as st:
        E = st.enter_context
        cId = E(nc.sbuf_tensor("sb_cId", [128, 128], BF16))
        cOnB = E(nc.sbuf_tensor("cOnB", [128, 128], BF16))
        cOnF = E(nc.sbuf_tensor("cOnF", [128, 128], F32))
        cEq = E(nc.sbuf_tensor("sb_cEq", [24, 32], F32))
        cEk = E(nc.sbuf_tensor("sb_cEk", [24, 32], F32))
        cHm = E(nc.sbuf_tensor("sb_cHm", [32, 8], F32))
        wq4t = E(nc.sbuf_tensor("wq4t", [128, 256], BF16))
        wk4t = E(nc.sbuf_tensor("wk4t", [128, 256], BF16))
        gatet = E(nc.sbuf_tensor("sb_gatet", [128, 64], BF16))
        AXs = E(nc.sbuf_tensor("AXs", [128, TT * 1024], F32))
        AFM = E(nc.sbuf_tensor("AFM", [128, 8 * T], BF16))
        AV = E(nc.sbuf_tensor("AV", [128, AVW], BF16))
        ASH = E(nc.sbuf_tensor("ASH", [128, 12288], BF16))
        osl = E(nc.sbuf_tensor("osl", [128, 4096], BF16))
        PT = E(nc.sbuf_tensor("PT", [128, 1536], BF16))
        qA = E(nc.sbuf_tensor("qA", [128, T], BF16))
        qB = E(nc.sbuf_tensor("qB", [128, T], BF16))
        kA = E(nc.sbuf_tensor("kA", [128, T], BF16))
        kB = E(nc.sbuf_tensor("kB", [128, T], BF16))
        sm = E(nc.sbuf_tensor("sm", [128, 64], F32))
        w_all = E(nc.sbuf_tensor("w_all", [128, TT * 8], F32))
        xc = E(nc.sbuf_tensor("xc", [128, 1024], BF16))
        s1b = E(nc.sbuf_tensor("s1b", [128, 1024], BF16))
        qsq = E(nc.sbuf_tensor("qsq", [32, 1024], F32))
        kn4 = E(nc.sbuf_tensor("kn4", [128, T], BF16))
        rr = E(nc.sbuf_tensor("rr", [1, 512], F32))
        recb = E(nc.sbuf_tensor("recb", [128, 512], F32))
        wstg = E(nc.sbuf_tensor("wstg", [128, 2048], F32))

        b0 = E(nc.psum_tensor("b0", [128, 512], F32))
        b1 = E(nc.psum_tensor("b1", [128, 512], F32))
        b2 = E(nc.psum_tensor("b2", [128, 512], F32))
        b3 = E(nc.psum_tensor("b3", [128, 512], F32))
        yA = E(nc.psum_tensor("yAps", [128, 1024], F32))
        b6 = E(nc.psum_tensor("b6", [128, 512], F32))
        tp = E(nc.psum_tensor("tp", [128, 1024], BF16))

        blk = E(nc.Block())
        P = Prog(nc, st)
        for snm in ("pe", "ve", "sc", "gp", "stC", "stC2", "stCW", "stPK",
                    "stPIa", "stPIb", "stX0", "stX1", "stX2", "stX3",
                    "stO0", "stO1", "stO2", "stO3"):
            P.new_sem(snm)

        def mm(out, lhsT, rhs, start, stop, tile_position=None, inc=None, waits=()):
            return P.op("tensor",
                        lambda e, o=out, l=lhsT, r=rhs, a=start, b=stop, t=tile_position:
                        e.matmul(out=o, lhsT=l, rhs=r, start=a, stop=b, tile_position=t),
                        inc=inc, waits=waits)

        def dma(qeng, out, in_, inc, waits=()):
            return P.op(qeng, lambda e, o=out, i=in_: e.dma_start(out=o, in_=i),
                        inc=inc, inc_by=16, waits=waits)

        def vop(fn, waits=()):
            P.op("vector", fn, waits=waits)
            return P.cnt["ve"]

        def sop(fn, waits=()):
            P.op("scalar", fn, waits=waits)
            return P.cnt["sc"]

        wstate = {"i": 0, "last": [("pe", 0), ("pe", 0)]}

        def wchunk(dst, src, waits=(), eng="scalar"):
            sl = wstate["i"] % 2
            wstate["i"] += 1
            psem = ("stPIa", "stPIb")[sl]
            dma("sync", wstg[:, sl * 1024:(sl + 1) * 1024], src, psem,
                waits=[wstate["last"][sl]])
            if eng == "scalar":
                P.op("scalar",
                     lambda e, d=dst, sl=sl: e.copy(
                         out=d, in_=wstg[:, sl * 1024:(sl + 1) * 1024]),
                     waits=[(psem, P.cnt[psem])] + list(waits))
                wstate["last"][sl] = ("sc", P.cnt["sc"])
                return P.cnt["sc"]
            P.op("vector",
                 lambda e, d=dst, sl=sl: e.tensor_copy(
                     out=d, in_=wstg[:, sl * 1024:(sl + 1) * 1024]),
                 waits=[(psem, P.cnt[psem])] + list(waits))
            wstate["last"][sl] = ("ve", P.cnt["ve"])
            return P.cnt["ve"]

        def ln_norm(tk):
            sl = AXs[:, tk * 1024:(tk + 1) * 1024]
            vop(lambda e: e.bn_stats(out=sm[:, 48:54], in_=sl[:, 0:512]))
            vop(lambda e: e.bn_stats(out=sm[:, 54:60], in_=sl[:, 512:1024]))
            vop(lambda e: e.bn_aggr(out=sm[:, 60:62], in_=sm[:, 48:60]))
            sop(lambda e: e.activation(out=sm[:, 5:6], in_=sm[:, 61:62],
                                       func=AF.Sqrt, bias=sm[:, 13:14]),
                waits=[("ve", P.cnt["ve"])])
            vop(lambda e: e.reciprocal(out=sm[:, 6:7], in_=sm[:, 5:6]),
                waits=[("sc", P.cnt["sc"])])
            vop(lambda e: e.tensor_scalar(out=sl, in0=sl, scalar1=sm[:, 60:61],
                                          scalar2=sm[:, 6:7], op0=ALU.subtract,
                                          op1=ALU.mult))

        def gate_chain(tk, gps, pe_gt):
            wsl = w_all[:, tk * 8:(tk + 1) * 8]
            vop(lambda e, gps=gps: e.tensor_copy(out=sm[:, 16:24], in_=gps[:, 0:8]),
                waits=[("pe", pe_gt)])
            vop(lambda e: e.reduce_max(out=sm[:, 8:9], in_=sm[:, 16:24], axis=AXL.X))
            vop(lambda e, wsl=wsl: e.tensor_scalar(out=wsl, in0=sm[:, 16:24],
                                                   scalar1=sm[:, 8:9],
                                                   scalar2=None, op0=ALU.is_equal))
            vop(lambda e, wsl=wsl: e.scalar_tensor_tensor(
                out=sm[:, 24:32], in0=wsl, scalar=NEG, in1=sm[:, 16:24],
                op0=ALU.mult, op1=ALU.add))
            vop(lambda e: e.reduce_max(out=sm[:, 9:10], in_=sm[:, 24:32], axis=AXL.X))
            vop(lambda e: e.tensor_tensor(out=sm[:, 10:11], in0=sm[:, 8:9],
                                          in1=sm[:, 9:10], op=ALU.subtract))
            sop(lambda e: e.activation(out=sm[:, 11:12], in_=sm[:, 10:11],
                                       func=AF.Sigmoid),
                waits=[("ve", P.cnt["ve"])])
            vop(lambda e: e.tensor_scalar(out=sm[:, 14:15], in0=sm[:, 11:12],
                                          scalar1=-1.0, scalar2=1.0, op0=ALU.mult,
                                          op1=ALU.add),
                waits=[("sc", P.cnt["sc"])])
            vop(lambda e: e.tensor_scalar(out=sm[:, 40:48], in0=sm[:, 24:32],
                                          scalar1=sm[:, 9:10], scalar2=None,
                                          op0=ALU.is_equal))
            vop(lambda e, wsl=wsl: e.tensor_scalar(out=wsl, in0=wsl,
                                                   scalar1=sm[:, 11:12],
                                                   scalar2=None, op0=ALU.mult))
            return vop(lambda e, wsl=wsl: e.scalar_tensor_tensor(
                out=wsl, in0=sm[:, 40:48], scalar=sm[:, 14:15], in1=wsl,
                op0=ALU.mult, op1=ALU.add))

        dma("sync", cOnF[:], cOn_h[:], "stC")
        dma("sync", cEq[:], cEq_h[:], "stC")
        dma("sync", cEk[:], cEk_h[:], "stC")
        dma("sync", cHm[:], cHm_h[:], "stC")
        dma("sync", sm[0:1, 62:63], sc_h[:], "stC")
        dma("sync", wstg[0:24, 0:1024], wq_h[:], "stC")
        stC_all = P.cnt["stC"]
        dma("sync", recb[:, 0:128], cId_h[:], "stC2")
        vop(lambda e: e.tensor_copy(out=cId[:], in_=recb[:, 0:128]),
            waits=[("stC2", P.cnt["stC2"])])
        dma("sync", recb[:, 0:64].rearrange("p (j e) -> p j e", e=8),
            gat_h.rearrange("j p e -> p j e"), "stC2",
            waits=[("ve", P.cnt["ve"])])
        vop(lambda e: e.tensor_copy(out=gatet[:], in_=recb[:, 0:64]),
            waits=[("stC2", P.cnt["stC2"])])
        vop(lambda e: e.tensor_copy(out=cOnB[:], in_=cOnF[:]), waits=[("stC", stC_all)])
        vop(lambda e: e.memset(sm[:, 13:14], 1e-5))
        P.op("gpsimd", lambda e: e.memset(kA[:], -1.0))
        P.op("gpsimd", lambda e: e.memset(kB[:], -1.0))
        gp_init = P.cnt["gp"]
        ve_consts = P.cnt["ve"]

        mm(b6[:, 0:1], cOnF[0:1, 0:128], sm[0:1, 62:63], True, True, inc="pe",
           waits=[("stC", stC_all), ("ve", ve_consts)])
        vop(lambda e: e.tensor_copy(out=sm[:, 12:13], in_=b6[:, 0:1]),
            waits=[("pe", P.cnt["pe"])])

        for dc in range(8):
            mm(b6[0:128, 0:32], wstg[0:24, dc * 128:(dc + 1) * 128], cEq[:], True, True,
               inc="pe", waits=[("ve", P.cnt["ve"])])
            vop(lambda e, dc=dc: e.tensor_copy(out=wq4t[:, dc * 32:(dc + 1) * 32],
                                               in_=b6[0:128, 0:32]),
                waits=[("pe", P.cnt["pe"])])
        dma("sync", wstg[0:24, 0:1024], wk_h[:], "stCW", waits=[("pe", P.cnt["pe"])])
        stC_wk = P.cnt["stCW"]
        for dc in range(8):
            mm(b6[0:128, 0:32], wstg[0:24, dc * 128:(dc + 1) * 128], cEk[:], True, True,
               inc="pe", waits=[("ve", P.cnt["ve"]), ("stCW", stC_wk)])
            vop(lambda e, dc=dc: e.tensor_copy(out=wk4t[:, dc * 32:(dc + 1) * 32],
                                               in_=b6[0:128, 0:32]),
                waits=[("pe", P.cnt["pe"])])
            vop(lambda e, dc=dc: e.tensor_scalar(
                out=wk4t[:, dc * 32:(dc + 1) * 32].rearrange("p (h i) -> p h i", i=4)[:, :, 0:3],
                in0=wk4t[:, dc * 32:(dc + 1) * 32].rearrange("p (h i) -> p h i", i=4)[:, :, 0:3],
                scalar1=sm[:, 12:13], scalar2=None, op0=ALU.mult))
        ve_wk = P.cnt["ve"]
        pe_qkbuild = P.cnt["pe"]
        wstate["last"] = [("pe", pe_qkbuild), ("pe", pe_qkbuild)]

        pe_wo_last = 0
        pe_qk_done = 0
        w12_idx = [0]
        marks = {"consts": P.t}

        for s in range(nseq):
            marks["s%d_start" % s] = P.t
            for g in range(4):
                dma("gpsimd",
                    AXs[:].rearrange("p (tt c) -> p tt c", c=1024)[:, g * 4:(g + 1) * 4, :],
                    x_h[s].rearrange("(tt p) c -> p tt c", p=128)[:, g * 4:(g + 1) * 4, :],
                    "stX%d" % g, waits=[("stO%d" % g, 64 * s)])
            aw = P.acquire("ASHvo")
            wv_done = []
            for j in range(8):
                wv_done.append(wchunk(ASH[:, j * 1024:(j + 1) * 1024], wvt_h[j],
                                      waits=aw if j == 0 else ()))

            afm_ve = {}
            osl_guard = P.acquire("oslw")
            qk_first = P.acquire("qkpack")
            for tk in range(TT):
                ve_cast = vop(lambda e, tk=tk: e.tensor_copy(
                    out=xc[:], in_=AXs[:, tk * 1024:(tk + 1) * 1024]),
                    waits=[("stX%d" % (tk // 4), 16 * (s + 1))] + P.acquire("xc"))
                for j in range(8):
                    w = ([("ve", ve_cast), ("stC", stC_all)] + P.acquire("tp")) if j == 0 else ()
                    P.op("tensor", lambda e, j=j: e.transpose(
                        out=tp[:, j * 128:(j + 1) * 128], in_=xc[:, j * 128:(j + 1) * 128],
                        identity=cId[:]), inc="pe" if j == 7 else None, waits=w)
                pe_tp = P.cnt["pe"]
                P.release("xc", "pe", pe_tp)
                ve_fm = vop(lambda e, tk=tk: e.tensor_copy(
                    out=AFM[:].rearrange("p (j t) -> p j t", t=T)[:, :, tk * 128:(tk + 1) * 128],
                    in_=tp[:].rearrange("p (j t) -> p j t", t=128)),
                    waits=[("pe", pe_tp)] + P.acquire("AFMw"))
                afm_ve[tk] = ve_fm
                P.release("tp", "ve", ve_fm)
                for j in range(8):
                    w = ([("ve", ve_fm)] +
                         P.acquire("yA0") + P.acquire("yA1")) if j == 0 else []
                    w = list(w) + [("sc", wv_done[j])]
                    mm(yA[:, 0:512], AFM[:, j * T + tk * 128: j * T + tk * 128 + 128],
                       ASH[:, j * 1024: j * 1024 + 512], j == 0, j == 7, waits=w)
                    mm(yA[:, 512:1024], AFM[:, j * T + tk * 128: j * T + tk * 128 + 128],
                       ASH[:, j * 1024 + 512: j * 1024 + 1024], j == 0, j == 7,
                       inc="pe" if j == 7 else None)
                pe_v = P.cnt["pe"]
                P.release("ASHvo", "pe", pe_v)
                ve_v = vop(lambda e, tk=tk: e.tensor_copy(
                    out=AV[:, tk * 1024:(tk + 1) * 1024], in_=yA[:]),
                    waits=[("pe", pe_v)] + P.acquire("AVw"))
                P.release("yA0", "ve", ve_v)
                P.release("yA1", "ve", ve_v)

                if tk % 4 != 3:
                    continue
                t4 = tk // 4
                for dc in range(8):
                    w = ([("ve", ve_fm)] + P.acquire("b6")) if dc == 0 else ()
                    mm(b6[0:32, 0:512], wq4t[:, dc * 32:(dc + 1) * 32],
                       AFM[:, dc * T + t4 * 512: dc * T + (t4 + 1) * 512],
                       dc == 0, dc == 7, inc="pe" if dc == 7 else None, waits=w)
                pe_q = P.cnt["pe"]
                sc_qsq = sop(lambda e: e.activation(out=qsq[:, 0:512],
                                                    in_=b6[0:32, 0:512], func=AF.Square),
                             waits=[("pe", pe_q)] + P.acquire("qsqA"))
                ve_qn = vop(lambda e, t4=t4: e.tensor_copy(
                    out=osl[0:32, t4 * 512:(t4 + 1) * 512], in_=b6[0:32, 0:512]),
                    waits=[("pe", pe_q), ("sc", sc_qsq)] +
                    (osl_guard if t4 == 0 else []))
                P.release("b6", "sc", sc_qsq)
                P.release("b6", "ve", ve_qn)
                for dc in range(8):
                    w = P.acquire("b2") if dc == 0 else ()
                    mm(b2[0:32, 0:512], wk4t[:, dc * 32:(dc + 1) * 32],
                       AFM[:, dc * T + t4 * 512: dc * T + (t4 + 1) * 512],
                       dc == 0, dc == 7, inc="pe" if dc == 7 else None, waits=w)
                pe_k = P.cnt["pe"]
                mm(b3[0:8, 0:512], cHm[:], qsq[:, 0:512], True, True, inc="pe",
                   waits=[("sc", sc_qsq)] + P.acquire("b3"))
                pe_qn = P.cnt["pe"]
                P.release("qsqA", "pe", pe_qn)
                sc_qn = sop(lambda e, t4=t4: e.activation(
                    out=osl[0:8, 2048 + t4 * 512: 2048 + (t4 + 1) * 512],
                    in_=b3[0:8, 0:512], func=AF.Sqrt),
                    waits=[("pe", pe_qn), ("ve", ve_qn)])
                P.release("b3", "sc", sc_qn)
                sc_ksq = sop(lambda e: e.activation(out=qsq[:, 512:1024],
                                                    in_=b2[0:32, 0:512], func=AF.Square),
                             waits=[("pe", pe_k)] + P.acquire("qsqB"))
                ve_kn = vop(lambda e, t4=t4: e.tensor_copy(
                    out=kn4[0:32, t4 * 512:(t4 + 1) * 512], in_=b2[0:32, 0:512]),
                    waits=[("pe", pe_k), ("sc", sc_ksq)])
                P.release("b2", "sc", sc_ksq)
                P.release("b2", "ve", ve_kn)
                mm(b6[0:8, 0:512], cHm[:], qsq[:, 512:1024], True, True, inc="pe",
                   waits=[("sc", sc_ksq)] + P.acquire("b6"))
                pe_kn = P.cnt["pe"]
                P.release("qsqB", "pe", pe_kn)
                ve_km = vop(lambda e, t4=t4: e.reduce_max(
                    out=sm[0:8, 34 + t4:35 + t4], in_=b6[0:8, 0:512], axis=AXL.X),
                    waits=[("pe", pe_kn)])
                P.release("b6", "ve", ve_km)
                pk_w = qk_first + [("gp", gp_init), ("ve", ve_kn)]
                for (dst, g0, srcb) in ((qA, 0, osl), (qB, 4, osl),
                                        (kA, 8, kn4), (kB, 12, kn4)):
                    for hl in range(4):
                        h = (g0 + hl) % 8

                        def f(e, dst=dst, srcb=srcb, h=h, hl=hl, t4=t4):
                            return e.dma_start(
                                out=dst[32 * hl:32 * hl + 4, t4 * 512:(t4 + 1) * 512],
                                in_=srcb[4 * h:4 * h + 4, t4 * 512:(t4 + 1) * 512])
                        P.op("gpsimd", f, inc="stPK", inc_by=16, waits=pk_w)
                        pk_w = ()
            ve_vdone = P.cnt["ve"]
            pe_vdone = P.cnt["pe"]

            vop(lambda e: e.reduce_max(out=sm[0:8, 32:33], in_=sm[0:8, 34:34 + T4],
                                       axis=AXL.X))
            sop(lambda e: e.activation(out=sm[0:8, 33:34], in_=sm[0:8, 32:33],
                                       func=AF.Sqrt), waits=[("ve", P.cnt["ve"])])
            ve_mb = vop(lambda e: e.tensor_scalar(
                out=osl[0:8, 2048:2048 + T], in0=osl[0:8, 2048:2048 + T],
                scalar1=sm[0:8, 33:34], scalar2=1.01, op0=ALU.mult, op1=ALU.mult),
                waits=[("sc", P.cnt["sc"])])
            pk_w = [("ve", ve_mb)]
            for h in range(8):
                dst = qA if h < 4 else qB
                hl = h % 4

                def f(e, dst=dst, h=h, hl=hl):
                    return e.dma_start(out=dst[32 * hl + 4:32 * hl + 5, :],
                                       in_=osl[h:h + 1, 2048:2048 + T])
                P.op("gpsimd", f, inc="stPK", inc_by=16, waits=pk_w)
                pk_w = ()
            stPK_now = P.cnt["stPK"]
            pe_qk_done = P.cnt["pe"]

            aw = [("pe", pe_vdone)] + P.acquire("ASHvo")
            wo_done = []
            for j in range(8):
                wo_done.append(wchunk(ASH[:, j * 1024:(j + 1) * 1024], wot_h[j],
                                      waits=aw if j == 0 else ()))

            sbanks = ((b0, "b0"), (b1, "b1"), (b3, "b3"))
            pending = []

            def flush_pending(pend):
                for tk in pend:
                    ve_cast = vop(lambda e, tk=tk: e.tensor_copy(
                        out=xc[:], in_=AXs[:, tk * 1024:(tk + 1) * 1024]),
                        waits=P.acquire("xc") + [("ve", ln_done[tk])])
                    for j in range(8):
                        w = ([("ve", ve_cast), ("pe", pe_qk_done)] +
                             P.acquire("tp")) if j == 0 else ()
                        P.op("tensor", lambda e, j=j: e.transpose(
                            out=tp[:, j * 128:(j + 1) * 128],
                            in_=xc[:, j * 128:(j + 1) * 128],
                            identity=cId[:]), inc="pe" if j == 7 else None, waits=w)
                    pe_tp = P.cnt["pe"]
                    P.release("xc", "pe", pe_tp)
                    ve_fm = vop(lambda e, tk=tk: e.tensor_copy(
                        out=AFM[:].rearrange("p (j t) -> p j t", t=T)[:, :, tk * 128:(tk + 1) * 128],
                        in_=tp[:].rearrange("p (j t) -> p j t", t=128)),
                        waits=[("pe", pe_tp)])
                    P.release("tp", "ve", ve_fm)
                    for dc in range(8):
                        w = ([("ve", ve_fm)] + P.acquire("yA0")) if dc == 0 else ()
                        mm(yA[:, 0:8], AFM[:, dc * T + tk * 128: dc * T + tk * 128 + 128],
                           gatet[:, dc * 8:(dc + 1) * 8], dc == 0, dc == 7,
                           inc="pe" if dc == 7 else None, waits=w)
                    pe_gt = P.cnt["pe"]
                    ve_g = gate_chain(tk, yA, pe_gt)
                    P.release("yA0", "ve", ve_g)

            marks["s%d_pass1" % s] = P.t
            ln_done = {}
            for t4 in range(T4):
                marks["s%d_p2t%d" % (s, t4)] = P.t
                for h in range(NH):
                    qT = qA if h < 4 else qB
                    kT = kA if h < 4 else kB
                    hl = h % 4
                    nss = 4 * t4 + 4
                    gate_at = {}
                    dbank, dnm = ((b6, "b6") if h % 2 == 0
                                  else (yA[:, 512:1024], "yA1"))

                    def den_o(ss2):
                        sl2 = ss2 % 3
                        first = ss2 == 0
                        last = ss2 == nss - 1
                        w = [gate_at[ss2]]
                        if first:
                            w += P.acquire(dnm)
                        mm(dbank[0:1, 0:512], cOnB[:, 0:1],
                           PT[:, sl2 * 512:(sl2 + 1) * 512],
                           first, last, inc="pe" if last else None, waits=w)
                        w2 = (P.acquire("b2") + [("ve", ve_vdone)]) if first else ()
                        mm(b2[:, 0:512],
                           AV[:, ss2 * 1024 + h * 128: ss2 * 1024 + h * 128 + 128],
                           PT[:, sl2 * 512:(sl2 + 1) * 512],
                           first, last, inc="pe", waits=w2)
                        P.release("PT%d" % sl2, "pe", P.cnt["pe"])

                    for ss in range(nss):
                        bsl, bnm = sbanks[ss % 3]
                        sl2 = ss % 3
                        di = ss - 4 * t4
                        w = [("stPK", stPK_now), ("gp", gp_init)] + P.acquire(bnm)
                        mm(bsl[:, 0:512], kT[32 * hl:32 * hl + 5, ss * 128:(ss + 1) * 128],
                           qT[32 * hl:32 * hl + 5, t4 * 512:(t4 + 1) * 512],
                           True, True, tile_position=(32 * hl, 0),
                           inc="pe", waits=w)
                        pe_g = P.cnt["pe"]
                        if ss >= 2:
                            den_o(ss - 2)
                        sc_e = sop(lambda e, sl2=sl2, bsl=bsl: e.activation(
                            out=PT[:, sl2 * 512:(sl2 + 1) * 512], in_=bsl[:, 0:512],
                            func=AF.Exp),
                            waits=[("pe", pe_g)] + P.acquire("PT%d" % sl2))
                        P.release(bnm, "sc", sc_e)
                        if di >= 0:
                            P.op("gpsimd", lambda e, sl2=sl2, di=di: e.affine_select(
                                PT[:, sl2 * 512:(sl2 + 1) * 512],
                                PT[:, sl2 * 512:(sl2 + 1) * 512],
                                [[1, 512]], ALU.is_ge, 0.0,
                                base=-128 * di, channel_multiplier=-1),
                                waits=[("sc", sc_e)])
                            gate_at[ss] = ("gp", P.cnt["gp"])
                        else:
                            gate_at[ss] = ("sc", sc_e)
                    den_o(nss - 2)
                    den_o(nss - 1)
                    pe_den = P.cnt["pe"]
                    ve_rr = vop(lambda e, dbank=dbank: e.reciprocal(
                        out=rr[:], in_=dbank[0:1, 0:512]),
                        waits=[("pe", pe_den)] + P.acquire("rr"))
                    mm(dbank[:, 0:512], cOnF[0:1, 0:128], rr[:], True, True, inc="pe",
                       waits=[("ve", ve_rr)])
                    pe_bc = P.cnt["pe"]
                    P.release("rr", "pe", pe_bc)
                    ve_rb = vop(lambda e, dbank=dbank: e.tensor_copy(
                        out=recb[:], in_=dbank[:, 0:512]),
                        waits=[("pe", pe_bc)] + P.acquire("recb"))
                    P.release(dnm, "ve", ve_rb)
                    ve_os = vop(lambda e, h=h: e.tensor_tensor(
                        out=osl[:, h * 512:(h + 1) * 512], in0=b2[:, 0:512], in1=recb[:],
                        op=ALU.mult),
                        waits=[("pe", pe_wo_last), ("stPK", stPK_now)])
                    P.release("b2", "ve", ve_os)
                    P.release("recb", "ve", ve_os)
                    if h % 2 == 1 and pending:
                        flush_pending(pending[:1])
                        pending = pending[1:]
                ve_oslab = P.cnt["ve"]

                for tk4 in range(4):
                    tk = t4 * 4 + tk4
                    for h in range(8):
                        w = ([("ve", ve_oslab)] +
                             P.acquire("yA0") + P.acquire("yA1")) if h == 0 else []
                        w = list(w) + [("sc", wo_done[h])]
                        mm(yA[:, 0:512], osl[:, h * 512 + tk4 * 128: h * 512 + tk4 * 128 + 128],
                           ASH[:, h * 1024: h * 1024 + 512], h == 0, h == 7, waits=w)
                        mm(yA[:, 512:1024], osl[:, h * 512 + tk4 * 128: h * 512 + tk4 * 128 + 128],
                           ASH[:, h * 1024 + 512: h * 1024 + 1024], h == 0, h == 7,
                           inc="pe" if h == 7 else None)
                    pe_wo = P.cnt["pe"]
                    pe_wo_last = pe_wo
                    ve_r1 = vop(lambda e, tk=tk: e.tensor_tensor(
                        out=AXs[:, tk * 1024:(tk + 1) * 1024], in0=yA[:],
                        in1=AXs[:, tk * 1024:(tk + 1) * 1024], op=ALU.add),
                        waits=[("pe", pe_wo)])
                    P.release("yA0", "ve", ve_r1)
                    P.release("yA1", "ve", ve_r1)
                    ln_norm(tk)
                    ln_done[tk] = P.cnt["ve"]
                    pending.append(tk)
            P.release("oslw", "pe", P.cnt["pe"])

            flush_pending(pending[:2])
            pend_rest = pending[2:]
            pending = []
            pe_ph2_prev = P.cnt["pe"]

            def load12(e_, qd, hcp):
                sl = w12_idx[0] % 2
                base = 8192 + sl * 2048
                w = P.acquire("w12s%d" % sl)
                last = 0
                for t2 in range(2):
                    hc = (0 if t2 == 0 else 16) + qd * 4 + hcp
                    last = wchunk(ASH[:, base + t2 * 1024: base + (t2 + 1) * 1024],
                                  w12t_h[e_, hc], waits=w if t2 == 0 else ())
                w12_idx[0] += 1
                return (sl, base, last)

            for e_ in range(ne):
                marks["s%d_moe_e%d" % (s, e_)] = P.t
                for qd in range(4):
                    h12_w12 = {}
                    for hcp in range(2):
                        h12_w12[hcp] = load12(e_, qd, hcp)
                    stW3_now = ("sc", 0)
                    i12 = 0
                    for hcp in range(4):
                        slw, base, tgt12 = h12_w12[hcp]
                        for ts in range(T4):
                            pair = i12 % 2
                            h1p, h2p = ((b2, b3) if pair == 0 else
                                        (yA[:, 0:512], yA[:, 512:1024]))
                            pn = ("b2", "b3") if pair == 0 else ("yA0", "yA1")
                            for dc in range(8):
                                w = ([("sc", tgt12)] + P.acquire(pn[0])) if dc == 0 else ()
                                mm(h1p[:, 0:512] if pair == 0 else h1p,
                                   ASH[:, base + dc * 128: base + dc * 128 + 128],
                                   AFM[:, dc * T + ts * 512:(dc * T + (ts + 1) * 512)],
                                   dc == 0, dc == 7, inc="pe" if dc == 7 else None,
                                   waits=w)
                            pe_h1 = P.cnt["pe"]
                            for dc in range(8):
                                w = P.acquire(pn[1]) if dc == 0 else ()
                                mm(h2p[:, 0:512] if pair == 0 else h2p,
                                   ASH[:, base + 1024 + dc * 128: base + 1024 + dc * 128 + 128],
                                   AFM[:, dc * T + ts * 512:(dc * T + (ts + 1) * 512)],
                                   dc == 0, dc == 7, inc="pe" if dc == 7 else None,
                                   waits=w)
                            pe_h2 = P.cnt["pe"]
                            ssl = i12 % 2
                            sc_si = sop(lambda e, h1p=h1p, pair=pair, ssl=ssl: e.activation(
                                out=s1b[:, ssl * 512:(ssl + 1) * 512],
                                in_=h1p[:, 0:512] if pair == 0 else h1p, func=AF.Silu),
                                waits=[("pe", pe_h1)] + P.acquire("s1b%d" % ssl))
                            P.release(pn[0], "sc", sc_si)
                            ve_hd = vop(lambda e, h2p=h2p, pair=pair, ssl=ssl, hcp=hcp, ts=ts:
                                        e.tensor_tensor(
                                out=AV[:, hcp * T + ts * 512: hcp * T + (ts + 1) * 512],
                                in0=h2p[:, 0:512] if pair == 0 else h2p,
                                in1=s1b[:, ssl * 512:(ssl + 1) * 512], op=ALU.mult),
                                waits=[("pe", pe_h2), ("sc", sc_si),
                                       ("pe", pe_ph2_prev)])
                            P.release(pn[1], "ve", ve_hd)
                            P.release("s1b%d" % ssl, "ve", ve_hd)
                            if ts == T4 - 1:
                                P.release("w12s%d" % slw, "ve", ve_hd)
                            i12 += 1
                            if pend_rest:
                                flush_pending(pend_rest[:1])
                                pend_rest = pend_rest[1:]
                        if hcp == 0:
                            w3w = (P.acquire("w3s") +
                                   ([("pe", pe_ph2_prev)]
                                    if (e_ == 0 and qd == 0) else []))
                            for c4 in range(4):
                                stW3_now = ("ve", wchunk(
                                    AV[:, W3B + c4 * 1024: W3B + (c4 + 1) * 1024],
                                    w3t_h[e_, qd * 4 + c4],
                                    waits=w3w if c4 == 0 else (), eng="vector"))
                        if hcp + 2 < 4:
                            h12_w12[hcp + 2] = load12(e_, qd, hcp + 2)
                    ve_hidden = P.cnt["ve"]
                    last_q = e_ == ne - 1 and qd == 3
                    for tk in range(TT):
                        yb = tk % 2
                        y0p, y1p = ((yA[:, 0:512], yA[:, 512:1024]) if yb == 0
                                    else (b0, b1))
                        ynm = ("yA0", "yA1") if yb == 0 else ("b0", "b1")
                        for hcp in range(4):
                            w = ([("ve", ve_hidden), stW3_now] +
                                 P.acquire(ynm[0]) + P.acquire(ynm[1])) if hcp == 0 else ()
                            mm(y0p[:, 0:512] if yb else y0p,
                               AV[:, hcp * T + tk * 128: hcp * T + tk * 128 + 128],
                               AV[:, W3B + hcp * 1024: W3B + hcp * 1024 + 512],
                               hcp == 0, hcp == 3, waits=w)
                            mm(y1p[:, 0:512] if yb else y1p,
                               AV[:, hcp * T + tk * 128: hcp * T + tk * 128 + 128],
                               AV[:, W3B + hcp * 1024 + 512: W3B + hcp * 1024 + 1024],
                               hcp == 0, hcp == 3, inc="pe" if hcp == 3 else None)
                        pe_y = P.cnt["pe"]
                        if yb == 0:
                            ve_y = vop(lambda e, tk=tk, e_=e_: e.scalar_tensor_tensor(
                                out=AXs[:, tk * 1024:(tk + 1) * 1024], in0=yA[:],
                                scalar=w_all[:, tk * 8 + e_: tk * 8 + e_ + 1],
                                in1=AXs[:, tk * 1024:(tk + 1) * 1024],
                                op0=ALU.mult, op1=ALU.add), waits=[("pe", pe_y)])
                        else:
                            vop(lambda e, tk=tk, e_=e_: e.scalar_tensor_tensor(
                                out=AXs[:, tk * 1024: tk * 1024 + 512], in0=b0[:, 0:512],
                                scalar=w_all[:, tk * 8 + e_: tk * 8 + e_ + 1],
                                in1=AXs[:, tk * 1024: tk * 1024 + 512],
                                op0=ALU.mult, op1=ALU.add), waits=[("pe", pe_y)])
                            ve_y = vop(lambda e, tk=tk, e_=e_: e.scalar_tensor_tensor(
                                out=AXs[:, tk * 1024 + 512:(tk + 1) * 1024], in0=b1[:, 0:512],
                                scalar=w_all[:, tk * 8 + e_: tk * 8 + e_ + 1],
                                in1=AXs[:, tk * 1024 + 512:(tk + 1) * 1024],
                                op0=ALU.mult, op1=ALU.add))
                        P.release(ynm[0], "ve", ve_y)
                        P.release(ynm[1], "ve", ve_y)
                        if last_q:
                            ln_norm(tk)
                            ve_n = P.cnt["ve"]
                            dma("gpsimd",
                                y_h[s].rearrange("(tt p) c -> p tt c", p=128)[:, tk:tk + 1, :],
                                AXs[:, tk * 1024:(tk + 1) * 1024].rearrange(
                                    "p (a c) -> p a c", a=1),
                                "stO%d" % (tk // 4), waits=[("ve", ve_n)])
                    pe_ph2_prev = P.cnt["pe"]
                    P.release("w3s", "pe", pe_ph2_prev)
            P.release("qkpack", "pe", P.cnt["pe"])
            P.release("ASHvo", "pe", P.cnt["pe"])
            P.release("oslw", "pe", P.cnt["pe"])
            P.release("AFMw", "pe", P.cnt["pe"])

        for g in range(4):
            P.t += 1
            P.q["gpsimd"].append(
                (P.t, lambda e, h=P.sems["stO%d" % g], v=P.cnt["stO%d" % g]:
                 e.wait_ge(h, v)))

        import os as _os
        lim = int(_os.environ.get("KERN_TRUNC", "0")) or None
        for ename in ENGS:
            ops = P.q[ename]

            def runner(eng, ops=ops, lim=lim):
                for t, f in ops:
                    if lim is not None and t > lim:
                        break
                    f(eng)

            getattr(blk, ename)(runner)
        nc._total_emission = P.t
        nc._marks = marks
    return nc


def make_inputs(x, wq, wk, wv, wo, scale, w12, w3, gate, nseq_core, ne=8):
    emat, hmat, ident, ones = host_consts()
    w12t = np.ascontiguousarray(
        w12.reshape(ne, 32, 128, 8, 128).transpose(0, 1, 4, 3, 2).reshape(ne, 32, 128, 1024))
    w3t = np.ascontiguousarray(
        w3.reshape(ne, 1024, 16, 128).transpose(0, 2, 3, 1).reshape(ne, 16, 128, 1024))
    wvt = np.ascontiguousarray(wv.T.reshape(8, 128, 1024))
    wot = np.ascontiguousarray(wo.T.reshape(8, 128, 1024))
    gatet = np.ascontiguousarray(gate.T.reshape(8, 128, 8))
    base = dict(wq=wq, wk=wk, wvt=wvt, wot=wot, scale=scale.reshape(1, 1),
                w12t=w12t, w3t=w3t, gatet=gatet,
                cEq=emat, cEk=emat, cHm=hmat, cId=ident, cOn=ones)
    maps = []
    for c in range(NCORES):
        m = dict(base)
        m["x"] = np.ascontiguousarray(x[c * nseq_core:(c + 1) * nseq_core])
        maps.append(m)
    return maps


def kernel(L0_attn_x, L0_attn_Wq, L0_attn_Wk, L0_attn_Wv, L0_attn_Wo, L0_attn_scale,
           L0_ffn_w12, L0_ffn_w3, L0_ffn_gate_w):
    f = lambda a: np.ascontiguousarray(np.asarray(a, dtype=np.float32))
    x = f(L0_attn_x)
    nseq_core = x.shape[0] // NCORES
    maps = make_inputs(x, f(L0_attn_Wq), f(L0_attn_Wk), f(L0_attn_Wv), f(L0_attn_Wo),
                       f(L0_attn_scale), f(L0_ffn_w12), f(L0_ffn_w3), f(L0_ffn_gate_w),
                       nseq_core)
    nc = build_program(nseq=nseq_core, T=x.shape[1])
    res = run_bass_kernel_spmd(nc, maps, core_ids=list(range(NCORES)))
    return np.concatenate([r["y"] for r in res.results], axis=0)


# version 08
# speedup vs baseline: 1.0857x; 1.0857x over previous
import itertools
import contextlib
import numpy as np

import concourse.bass as bass
import concourse.mybir as mybir
from concourse.bass_utils import run_bass_kernel_spmd

F32 = mybir.dt.float32
BF16 = mybir.dt.bfloat16
AF = mybir.ActivationFunctionType
ALU = mybir.AluOpType
AXL = mybir.AxisListType

NCORES = 8
D = 1024
NH = 8
NEG = -1.0e30

ENGS = ("sync", "tensor", "vector", "scalar", "gpsimd")
SELFCHAIN = {"vector": "ve", "scalar": "sc", "gpsimd": "gp"}


class Prog:

    def __init__(self, nc, stack):
        self.nc = nc
        self.stack = stack
        self.sems = {}
        self.cnt = {}
        self.q = {e: [] for e in ENGS}
        self.seen = {e: {} for e in ENGS}
        self.res = {}
        self.t = 0

    def new_sem(self, name):
        self.sems[name] = self.stack.enter_context(self.nc.semaphore(name))
        self.cnt[name] = 0
        return name

    def wait(self, eng, sname, val):
        if val is None or val <= 0 or self.seen[eng].get(sname, -1) >= val:
            return
        self.seen[eng][sname] = val
        h = self.sems[sname]
        self.t += 1
        self.q[eng].append((self.t, lambda e, h=h, v=val: e.wait_ge(h, v)))

    def op(self, eng, fn, inc=None, inc_by=1, waits=()):
        own = SELFCHAIN.get(eng)
        if own is not None:
            self.wait(eng, own, self.cnt[own])
        for w in waits:
            if w is None:
                continue
            self.wait(eng, w[0], w[1])
        if own is not None and inc is None:
            inc = own
        if inc is not None:
            h = self.sems[inc]
            self.cnt[inc] += inc_by

            def run(e, fn=fn, h=h, b=inc_by):
                fn(e).then_inc(h, b)

            self.t += 1
            self.q[eng].append((self.t, run))
            return self.cnt[inc]
        self.t += 1
        self.q[eng].append((self.t, fn))
        return None

    def acquire(self, name):
        w = self.res.get(name, [])
        self.res[name] = []
        return w

    def release(self, name, sem, cntv):
        self.res.setdefault(name, []).append((sem, cntv))


def host_consts():
    signs = [-1.0, 1.0]
    tri = np.array(list(itertools.product(signs, repeat=3)), dtype=np.float32)
    tri = tri / np.linalg.norm(tri, axis=1, keepdims=True)
    dirs = tri[:NH]
    emat = np.zeros((24, 32), np.float32)
    for h in range(8):
        for i in range(3):
            emat[3 * h + i, 4 * h + i] = 1.0
            emat[3 * h + i, 4 * h + 3] = dirs[h, i]
    hmat = np.zeros((32, 8), np.float32)
    for h in range(8):
        hmat[4 * h:4 * h + 4, h] = 1.0
    ident = np.eye(128, dtype=np.float32)
    ones = np.ones((128, 128), np.float32)
    return emat, hmat, ident, ones


def build_program(nseq=4, T=2048, ne=8):
    TT = T // 128
    T4 = T // 512
    AVW = TT * 1024
    W3B = 4 * T
    nc = bass.Bass("TRN2", target_bir_lowering=False)

    x_h = nc.dram_tensor("x", [nseq, T, D], F32, kind="ExternalInput").ap()
    y_h = nc.dram_tensor("y", [nseq, T, D], F32, kind="ExternalOutput").ap()
    wq_h = nc.dram_tensor("wq", [24, D], F32, kind="ExternalInput").ap()
    wk_h = nc.dram_tensor("wk", [24, D], F32, kind="ExternalInput").ap()
    wvt_h = nc.dram_tensor("wvt", [8, 128, D], F32, kind="ExternalInput").ap()
    wot_h = nc.dram_tensor("wot", [8, 128, D], F32, kind="ExternalInput").ap()
    sc_h = nc.dram_tensor("scale", [1, 1], F32, kind="ExternalInput").ap()
    w12t_h = nc.dram_tensor("w12t", [ne, 32, 128, D], F32, kind="ExternalInput").ap()
    w3t_h = nc.dram_tensor("w3t", [ne, 16, 128, D], F32, kind="ExternalInput").ap()
    gat_h = nc.dram_tensor("gatet", [8, 128, 8], F32, kind="ExternalInput").ap()
    cEq_h = nc.dram_tensor("cEq", [24, 32], F32, kind="ExternalInput").ap()
    cEk_h = nc.dram_tensor("cEk", [24, 32], F32, kind="ExternalInput").ap()
    cHm_h = nc.dram_tensor("cHm", [32, 8], F32, kind="ExternalInput").ap()
    cId_h = nc.dram_tensor("cId", [128, 128], F32, kind="ExternalInput").ap()
    cOn_h = nc.dram_tensor("cOn", [128, 128], F32, kind="ExternalInput").ap()

    with contextlib.ExitStack() as st:
        E = st.enter_context
        cId = E(nc.sbuf_tensor("sb_cId", [128, 128], BF16))
        cOnB = E(nc.sbuf_tensor("cOnB", [128, 128], BF16))
        cOnF = E(nc.sbuf_tensor("cOnF", [128, 128], F32))
        cEq = E(nc.sbuf_tensor("sb_cEq", [24, 32], F32))
        cEk = E(nc.sbuf_tensor("sb_cEk", [24, 32], F32))
        cHm = E(nc.sbuf_tensor("sb_cHm", [32, 8], F32))
        wq4t = E(nc.sbuf_tensor("wq4t", [128, 256], BF16))
        wk4t = E(nc.sbuf_tensor("wk4t", [128, 256], BF16))
        gatet = E(nc.sbuf_tensor("sb_gatet", [128, 64], BF16))
        AXs = E(nc.sbuf_tensor("AXs", [128, TT * 1024], F32))
        AFM = E(nc.sbuf_tensor("AFM", [128, 8 * T], BF16))
        AV = E(nc.sbuf_tensor("AV", [128, AVW], BF16))
        ASH = E(nc.sbuf_tensor("ASH", [128, 12288], BF16))
        osl = E(nc.sbuf_tensor("osl", [128, 4096], BF16))
        PT = E(nc.sbuf_tensor("PT", [128, 2048], BF16))
        qA = E(nc.sbuf_tensor("qA", [128, T], BF16))
        qB = E(nc.sbuf_tensor("qB", [128, T], BF16))
        kA = E(nc.sbuf_tensor("kA", [128, T], BF16))
        kB = E(nc.sbuf_tensor("kB", [128, T], BF16))
        sm = E(nc.sbuf_tensor("sm", [128, 64], F32))
        w_all = E(nc.sbuf_tensor("w_all", [128, TT * 8], F32))
        xc = E(nc.sbuf_tensor("xc", [128, 1024], BF16))
        s1b = E(nc.sbuf_tensor("s1b", [128, 1024], BF16))
        qsq = E(nc.sbuf_tensor("qsq", [32, 1024], F32))
        kn4 = E(nc.sbuf_tensor("kn4", [128, T], BF16))
        rr = E(nc.sbuf_tensor("rr", [1, 512], F32))
        recb = E(nc.sbuf_tensor("recb", [128, 512], F32))
        wstg = E(nc.sbuf_tensor("wstg", [128, 2048], F32))

        b0 = E(nc.psum_tensor("b0", [128, 512], F32))
        b1 = E(nc.psum_tensor("b1", [128, 512], F32))
        b2 = E(nc.psum_tensor("b2", [128, 512], F32))
        b3 = E(nc.psum_tensor("b3", [128, 512], F32))
        yA = E(nc.psum_tensor("yAps", [128, 1024], F32))
        b6 = E(nc.psum_tensor("b6", [128, 512], F32))
        tp = E(nc.psum_tensor("tp", [128, 1024], BF16))

        blk = E(nc.Block())
        P = Prog(nc, st)
        for snm in ("pe", "ve", "sc", "gp", "stC", "stC2", "stCW", "stPK",
                    "stPIa", "stPIb", "stX0", "stX1", "stX2", "stX3",
                    "stO0", "stO1", "stO2", "stO3"):
            P.new_sem(snm)

        def mm(out, lhsT, rhs, start, stop, tile_position=None, inc=None, waits=()):
            return P.op("tensor",
                        lambda e, o=out, l=lhsT, r=rhs, a=start, b=stop, t=tile_position:
                        e.matmul(out=o, lhsT=l, rhs=r, start=a, stop=b, tile_position=t),
                        inc=inc, waits=waits)

        def dma(qeng, out, in_, inc, waits=()):
            return P.op(qeng, lambda e, o=out, i=in_: e.dma_start(out=o, in_=i),
                        inc=inc, inc_by=16, waits=waits)

        def vop(fn, waits=()):
            P.op("vector", fn, waits=waits)
            return P.cnt["ve"]

        def sop(fn, waits=()):
            P.op("scalar", fn, waits=waits)
            return P.cnt["sc"]

        wstate = {"i": 0, "last": [("pe", 0), ("pe", 0)]}

        def wchunk(dst, src, waits=(), eng="scalar"):
            sl = wstate["i"] % 2
            wstate["i"] += 1
            psem = ("stPIa", "stPIb")[sl]
            dma("sync", wstg[:, sl * 1024:(sl + 1) * 1024], src, psem,
                waits=[wstate["last"][sl]])
            if eng == "scalar":
                P.op("scalar",
                     lambda e, d=dst, sl=sl: e.copy(
                         out=d, in_=wstg[:, sl * 1024:(sl + 1) * 1024]),
                     waits=[(psem, P.cnt[psem])] + list(waits))
                wstate["last"][sl] = ("sc", P.cnt["sc"])
                return P.cnt["sc"]
            P.op("vector",
                 lambda e, d=dst, sl=sl: e.tensor_copy(
                     out=d, in_=wstg[:, sl * 1024:(sl + 1) * 1024]),
                 waits=[(psem, P.cnt[psem])] + list(waits))
            wstate["last"][sl] = ("ve", P.cnt["ve"])
            return P.cnt["ve"]

        def ln_norm(tk):
            sl = AXs[:, tk * 1024:(tk + 1) * 1024]
            vop(lambda e: e.bn_stats(out=sm[:, 48:54], in_=sl[:, 0:512]))
            vop(lambda e: e.bn_stats(out=sm[:, 54:60], in_=sl[:, 512:1024]))
            vop(lambda e: e.bn_aggr(out=sm[:, 60:62], in_=sm[:, 48:60]))
            sop(lambda e: e.activation(out=sm[:, 5:6], in_=sm[:, 61:62],
                                       func=AF.Sqrt, bias=sm[:, 13:14]),
                waits=[("ve", P.cnt["ve"])])
            vop(lambda e: e.reciprocal(out=sm[:, 6:7], in_=sm[:, 5:6]),
                waits=[("sc", P.cnt["sc"])])
            vop(lambda e: e.tensor_scalar(out=sl, in0=sl, scalar1=sm[:, 60:61],
                                          scalar2=sm[:, 6:7], op0=ALU.subtract,
                                          op1=ALU.mult))

        def gate_chain(tk, gps, pe_gt):
            wsl = w_all[:, tk * 8:(tk + 1) * 8]
            vop(lambda e, gps=gps: e.tensor_copy(out=sm[:, 16:24], in_=gps[:, 0:8]),
                waits=[("pe", pe_gt)])
            vop(lambda e: e.reduce_max(out=sm[:, 8:9], in_=sm[:, 16:24], axis=AXL.X))
            vop(lambda e, wsl=wsl: e.tensor_scalar(out=wsl, in0=sm[:, 16:24],
                                                   scalar1=sm[:, 8:9],
                                                   scalar2=None, op0=ALU.is_equal))
            vop(lambda e, wsl=wsl: e.scalar_tensor_tensor(
                out=sm[:, 24:32], in0=wsl, scalar=NEG, in1=sm[:, 16:24],
                op0=ALU.mult, op1=ALU.add))
            vop(lambda e: e.reduce_max(out=sm[:, 9:10], in_=sm[:, 24:32], axis=AXL.X))
            vop(lambda e: e.tensor_tensor(out=sm[:, 10:11], in0=sm[:, 8:9],
                                          in1=sm[:, 9:10], op=ALU.subtract))
            sop(lambda e: e.activation(out=sm[:, 11:12], in_=sm[:, 10:11],
                                       func=AF.Sigmoid),
                waits=[("ve", P.cnt["ve"])])
            vop(lambda e: e.tensor_scalar(out=sm[:, 14:15], in0=sm[:, 11:12],
                                          scalar1=-1.0, scalar2=1.0, op0=ALU.mult,
                                          op1=ALU.add),
                waits=[("sc", P.cnt["sc"])])
            vop(lambda e: e.tensor_scalar(out=sm[:, 40:48], in0=sm[:, 24:32],
                                          scalar1=sm[:, 9:10], scalar2=None,
                                          op0=ALU.is_equal))
            vop(lambda e, wsl=wsl: e.tensor_scalar(out=wsl, in0=wsl,
                                                   scalar1=sm[:, 11:12],
                                                   scalar2=None, op0=ALU.mult))
            return vop(lambda e, wsl=wsl: e.scalar_tensor_tensor(
                out=wsl, in0=sm[:, 40:48], scalar=sm[:, 14:15], in1=wsl,
                op0=ALU.mult, op1=ALU.add))

        dma("sync", cOnF[:], cOn_h[:], "stC")
        dma("sync", cEq[:], cEq_h[:], "stC")
        dma("sync", cEk[:], cEk_h[:], "stC")
        dma("sync", cHm[:], cHm_h[:], "stC")
        dma("sync", sm[0:1, 62:63], sc_h[:], "stC")
        dma("sync", wstg[0:24, 0:1024], wq_h[:], "stC")
        stC_all = P.cnt["stC"]
        dma("sync", recb[:, 0:128], cId_h[:], "stC2")
        vop(lambda e: e.tensor_copy(out=cId[:], in_=recb[:, 0:128]),
            waits=[("stC2", P.cnt["stC2"])])
        dma("sync", recb[:, 0:64].rearrange("p (j e) -> p j e", e=8),
            gat_h.rearrange("j p e -> p j e"), "stC2",
            waits=[("ve", P.cnt["ve"])])
        vop(lambda e: e.tensor_copy(out=gatet[:], in_=recb[:, 0:64]),
            waits=[("stC2", P.cnt["stC2"])])
        vop(lambda e: e.tensor_copy(out=cOnB[:], in_=cOnF[:]), waits=[("stC", stC_all)])
        vop(lambda e: e.memset(sm[:, 13:14], 1e-5))
        P.op("gpsimd", lambda e: e.memset(kA[:], -1.0))
        P.op("gpsimd", lambda e: e.memset(kB[:], -1.0))
        gp_init = P.cnt["gp"]
        ve_consts = P.cnt["ve"]

        mm(b6[:, 0:1], cOnF[0:1, 0:128], sm[0:1, 62:63], True, True, inc="pe",
           waits=[("stC", stC_all), ("ve", ve_consts)])
        vop(lambda e: e.tensor_copy(out=sm[:, 12:13], in_=b6[:, 0:1]),
            waits=[("pe", P.cnt["pe"])])

        for dc in range(8):
            mm(b6[0:128, 0:32], wstg[0:24, dc * 128:(dc + 1) * 128], cEq[:], True, True,
               inc="pe", waits=[("ve", P.cnt["ve"])])
            vop(lambda e, dc=dc: e.tensor_copy(out=wq4t[:, dc * 32:(dc + 1) * 32],
                                               in_=b6[0:128, 0:32]),
                waits=[("pe", P.cnt["pe"])])
        dma("sync", wstg[0:24, 0:1024], wk_h[:], "stCW", waits=[("pe", P.cnt["pe"])])
        stC_wk = P.cnt["stCW"]
        for dc in range(8):
            mm(b6[0:128, 0:32], wstg[0:24, dc * 128:(dc + 1) * 128], cEk[:], True, True,
               inc="pe", waits=[("ve", P.cnt["ve"]), ("stCW", stC_wk)])
            vop(lambda e, dc=dc: e.tensor_copy(out=wk4t[:, dc * 32:(dc + 1) * 32],
                                               in_=b6[0:128, 0:32]),
                waits=[("pe", P.cnt["pe"])])
            vop(lambda e, dc=dc: e.tensor_scalar(
                out=wk4t[:, dc * 32:(dc + 1) * 32].rearrange("p (h i) -> p h i", i=4)[:, :, 0:3],
                in0=wk4t[:, dc * 32:(dc + 1) * 32].rearrange("p (h i) -> p h i", i=4)[:, :, 0:3],
                scalar1=sm[:, 12:13], scalar2=None, op0=ALU.mult))
        ve_wk = P.cnt["ve"]
        pe_qkbuild = P.cnt["pe"]
        wstate["last"] = [("pe", pe_qkbuild), ("pe", pe_qkbuild)]

        pe_wo_last = 0
        pe_qk_done = 0
        w12_idx = [0]
        marks = {"consts": P.t}

        for s in range(nseq):
            marks["s%d_start" % s] = P.t
            for g in range(4):
                dma("gpsimd",
                    AXs[:].rearrange("p (tt c) -> p tt c", c=1024)[:, g * 4:(g + 1) * 4, :],
                    x_h[s].rearrange("(tt p) c -> p tt c", p=128)[:, g * 4:(g + 1) * 4, :],
                    "stX%d" % g, waits=[("stO%d" % g, 64 * s)])
            aw = P.acquire("ASHvo")
            wv_done = []
            for j in range(8):
                wv_done.append(wchunk(ASH[:, j * 1024:(j + 1) * 1024], wvt_h[j],
                                      waits=aw if j == 0 else ()))

            afm_ve = {}
            osl_guard = P.acquire("oslw")
            qk_first = P.acquire("qkpack")
            for tk in range(TT):
                ve_cast = vop(lambda e, tk=tk: e.tensor_copy(
                    out=xc[:], in_=AXs[:, tk * 1024:(tk + 1) * 1024]),
                    waits=[("stX%d" % (tk // 4), 16 * (s + 1))] + P.acquire("xc"))
                for j in range(8):
                    w = ([("ve", ve_cast), ("stC", stC_all)] + P.acquire("tp")) if j == 0 else ()
                    P.op("tensor", lambda e, j=j: e.transpose(
                        out=tp[:, j * 128:(j + 1) * 128], in_=xc[:, j * 128:(j + 1) * 128],
                        identity=cId[:]), inc="pe" if j == 7 else None, waits=w)
                pe_tp = P.cnt["pe"]
                P.release("xc", "pe", pe_tp)
                ve_fm = vop(lambda e, tk=tk: e.tensor_copy(
                    out=AFM[:].rearrange("p (j t) -> p j t", t=T)[:, :, tk * 128:(tk + 1) * 128],
                    in_=tp[:].rearrange("p (j t) -> p j t", t=128)),
                    waits=[("pe", pe_tp)] + P.acquire("AFMw"))
                afm_ve[tk] = ve_fm
                P.release("tp", "ve", ve_fm)
                for j in range(8):
                    w = ([("ve", ve_fm)] +
                         P.acquire("yA0") + P.acquire("yA1")) if j == 0 else []
                    w = list(w) + [("sc", wv_done[j])]
                    mm(yA[:, 0:512], AFM[:, j * T + tk * 128: j * T + tk * 128 + 128],
                       ASH[:, j * 1024: j * 1024 + 512], j == 0, j == 7, waits=w)
                    mm(yA[:, 512:1024], AFM[:, j * T + tk * 128: j * T + tk * 128 + 128],
                       ASH[:, j * 1024 + 512: j * 1024 + 1024], j == 0, j == 7,
                       inc="pe" if j == 7 else None)
                pe_v = P.cnt["pe"]
                P.release("ASHvo", "pe", pe_v)
                ve_v = vop(lambda e, tk=tk: e.tensor_copy(
                    out=AV[:, tk * 1024:(tk + 1) * 1024], in_=yA[:]),
                    waits=[("pe", pe_v)] + P.acquire("AVw"))
                P.release("yA0", "ve", ve_v)
                P.release("yA1", "ve", ve_v)

                if tk % 4 != 3:
                    continue
                t4 = tk // 4
                for dc in range(8):
                    w = ([("ve", ve_fm)] + P.acquire("b6")) if dc == 0 else ()
                    mm(b6[0:32, 0:512], wq4t[:, dc * 32:(dc + 1) * 32],
                       AFM[:, dc * T + t4 * 512: dc * T + (t4 + 1) * 512],
                       dc == 0, dc == 7, inc="pe" if dc == 7 else None, waits=w)
                pe_q = P.cnt["pe"]
                sc_qsq = sop(lambda e: e.activation(out=qsq[:, 0:512],
                                                    in_=b6[0:32, 0:512], func=AF.Square),
                             waits=[("pe", pe_q)] + P.acquire("qsqA"))
                ve_qn = vop(lambda e, t4=t4: e.tensor_copy(
                    out=osl[0:32, t4 * 512:(t4 + 1) * 512], in_=b6[0:32, 0:512]),
                    waits=[("pe", pe_q), ("sc", sc_qsq)] +
                    (osl_guard if t4 == 0 else []))
                P.release("b6", "sc", sc_qsq)
                P.release("b6", "ve", ve_qn)
                for dc in range(8):
                    w = P.acquire("b2") if dc == 0 else ()
                    mm(b2[0:32, 0:512], wk4t[:, dc * 32:(dc + 1) * 32],
                       AFM[:, dc * T + t4 * 512: dc * T + (t4 + 1) * 512],
                       dc == 0, dc == 7, inc="pe" if dc == 7 else None, waits=w)
                pe_k = P.cnt["pe"]
                mm(b3[0:8, 0:512], cHm[:], qsq[:, 0:512], True, True, inc="pe",
                   waits=[("sc", sc_qsq)] + P.acquire("b3"))
                pe_qn = P.cnt["pe"]
                P.release("qsqA", "pe", pe_qn)
                sc_qn = sop(lambda e, t4=t4: e.activation(
                    out=osl[0:8, 2048 + t4 * 512: 2048 + (t4 + 1) * 512],
                    in_=b3[0:8, 0:512], func=AF.Sqrt),
                    waits=[("pe", pe_qn), ("ve", ve_qn)])
                P.release("b3", "sc", sc_qn)
                sc_ksq = sop(lambda e: e.activation(out=qsq[:, 512:1024],
                                                    in_=b2[0:32, 0:512], func=AF.Square),
                             waits=[("pe", pe_k)] + P.acquire("qsqB"))
                ve_kn = vop(lambda e, t4=t4: e.tensor_copy(
                    out=kn4[0:32, t4 * 512:(t4 + 1) * 512], in_=b2[0:32, 0:512]),
                    waits=[("pe", pe_k), ("sc", sc_ksq)])
                P.release("b2", "sc", sc_ksq)
                P.release("b2", "ve", ve_kn)
                mm(b6[0:8, 0:512], cHm[:], qsq[:, 512:1024], True, True, inc="pe",
                   waits=[("sc", sc_ksq)] + P.acquire("b6"))
                pe_kn = P.cnt["pe"]
                P.release("qsqB", "pe", pe_kn)
                ve_km = vop(lambda e, t4=t4: e.reduce_max(
                    out=sm[0:8, 34 + t4:35 + t4], in_=b6[0:8, 0:512], axis=AXL.X),
                    waits=[("pe", pe_kn)])
                P.release("b6", "ve", ve_km)
                pk_w = qk_first + [("gp", gp_init), ("ve", ve_kn)]
                for (dst, g0, srcb) in ((qA, 0, osl), (qB, 4, osl),
                                        (kA, 8, kn4), (kB, 12, kn4)):
                    for hl in range(4):
                        h = (g0 + hl) % 8

                        def f(e, dst=dst, srcb=srcb, h=h, hl=hl, t4=t4):
                            return e.dma_start(
                                out=dst[32 * hl:32 * hl + 4, t4 * 512:(t4 + 1) * 512],
                                in_=srcb[4 * h:4 * h + 4, t4 * 512:(t4 + 1) * 512])
                        P.op("gpsimd", f, inc="stPK", inc_by=16, waits=pk_w)
                        pk_w = ()
            ve_vdone = P.cnt["ve"]
            pe_vdone = P.cnt["pe"]

            vop(lambda e: e.reduce_max(out=sm[0:8, 32:33], in_=sm[0:8, 34:34 + T4],
                                       axis=AXL.X))
            sop(lambda e: e.activation(out=sm[0:8, 33:34], in_=sm[0:8, 32:33],
                                       func=AF.Sqrt), waits=[("ve", P.cnt["ve"])])
            ve_mb = vop(lambda e: e.tensor_scalar(
                out=osl[0:8, 2048:2048 + T], in0=osl[0:8, 2048:2048 + T],
                scalar1=sm[0:8, 33:34], scalar2=1.01, op0=ALU.mult, op1=ALU.mult),
                waits=[("sc", P.cnt["sc"])])
            pk_w = [("ve", ve_mb)]
            for h in range(8):
                dst = qA if h < 4 else qB
                hl = h % 4

                def f(e, dst=dst, h=h, hl=hl):
                    return e.dma_start(out=dst[32 * hl + 4:32 * hl + 5, :],
                                       in_=osl[h:h + 1, 2048:2048 + T])
                P.op("gpsimd", f, inc="stPK", inc_by=16, waits=pk_w)
                pk_w = ()
            stPK_now = P.cnt["stPK"]
            pe_qk_done = P.cnt["pe"]

            aw = [("pe", pe_vdone)] + P.acquire("ASHvo")
            wo_done = []
            for j in range(8):
                wo_done.append(wchunk(ASH[:, j * 1024:(j + 1) * 1024], wot_h[j],
                                      waits=aw if j == 0 else ()))

            sbanks = ((b0, "b0"), (b1, "b1"), (b3, "b3"))
            pending = []

            def flush_pending(pend):
                for tk in pend:
                    ve_cast = vop(lambda e, tk=tk: e.tensor_copy(
                        out=xc[:], in_=AXs[:, tk * 1024:(tk + 1) * 1024]),
                        waits=P.acquire("xc") + [("ve", ln_done[tk])])
                    for j in range(8):
                        w = ([("ve", ve_cast), ("pe", pe_qk_done)] +
                             P.acquire("tp")) if j == 0 else ()
                        P.op("tensor", lambda e, j=j: e.transpose(
                            out=tp[:, j * 128:(j + 1) * 128],
                            in_=xc[:, j * 128:(j + 1) * 128],
                            identity=cId[:]), inc="pe" if j == 7 else None, waits=w)
                    pe_tp = P.cnt["pe"]
                    P.release("xc", "pe", pe_tp)
                    ve_fm = vop(lambda e, tk=tk: e.tensor_copy(
                        out=AFM[:].rearrange("p (j t) -> p j t", t=T)[:, :, tk * 128:(tk + 1) * 128],
                        in_=tp[:].rearrange("p (j t) -> p j t", t=128)),
                        waits=[("pe", pe_tp)])
                    P.release("tp", "ve", ve_fm)
                    for dc in range(8):
                        w = ([("ve", ve_fm)] + P.acquire("yA0")) if dc == 0 else ()
                        mm(yA[:, 0:8], AFM[:, dc * T + tk * 128: dc * T + tk * 128 + 128],
                           gatet[:, dc * 8:(dc + 1) * 8], dc == 0, dc == 7,
                           inc="pe" if dc == 7 else None, waits=w)
                    pe_gt = P.cnt["pe"]
                    ve_g = gate_chain(tk, yA, pe_gt)
                    P.release("yA0", "ve", ve_g)

            marks["s%d_pass1" % s] = P.t
            ln_done = {}
            for t4 in range(T4):
                marks["s%d_p2t%d" % (s, t4)] = P.t
                tail_ops = []
                for h in range(NH):
                    qT = qA if h < 4 else qB
                    kT = kA if h < 4 else kB
                    hl = h % 4
                    nss = 4 * t4 + 4
                    gate_at = {}
                    dbank, dnm = ((b6, "b6") if h % 2 == 0
                                  else (yA[:, 512:1024], "yA1"))

                    def den_o(ss2):
                        sl2 = ss2 % 4
                        first = ss2 == 0
                        last = ss2 == nss - 1
                        w = [gate_at[ss2]]
                        if first:
                            w += P.acquire(dnm)
                        mm(dbank[0:1, 0:512], cOnB[:, 0:1],
                           PT[:, sl2 * 512:(sl2 + 1) * 512],
                           first, last, inc="pe" if last else None, waits=w)
                        w2 = (P.acquire("b2") + [("ve", ve_vdone)]) if first else ()
                        mm(b2[:, 0:512],
                           AV[:, ss2 * 1024 + h * 128: ss2 * 1024 + h * 128 + 128],
                           PT[:, sl2 * 512:(sl2 + 1) * 512],
                           first, last, inc="pe", waits=w2)
                        P.release("PT%d" % sl2, "pe", P.cnt["pe"])

                    for ss in range(nss):
                        bsl, bnm = sbanks[ss % 3]
                        sl2 = ss % 4
                        di = ss - 4 * t4
                        w = [("stPK", stPK_now), ("gp", gp_init)] + P.acquire(bnm)
                        mm(bsl[:, 0:512], kT[32 * hl:32 * hl + 5, ss * 128:(ss + 1) * 128],
                           qT[32 * hl:32 * hl + 5, t4 * 512:(t4 + 1) * 512],
                           True, True, tile_position=(32 * hl, 0),
                           inc="pe", waits=w)
                        pe_g = P.cnt["pe"]
                        if ss == 1 and tail_ops:
                            for f in tail_ops:
                                f()
                            tail_ops = []
                        if ss >= 3:
                            den_o(ss - 3)
                        sc_e = sop(lambda e, sl2=sl2, bsl=bsl: e.activation(
                            out=PT[:, sl2 * 512:(sl2 + 1) * 512], in_=bsl[:, 0:512],
                            func=AF.Exp),
                            waits=[("pe", pe_g)] + P.acquire("PT%d" % sl2))
                        P.release(bnm, "sc", sc_e)
                        if di >= 0:
                            P.op("gpsimd", lambda e, sl2=sl2, di=di: e.affine_select(
                                PT[:, sl2 * 512:(sl2 + 1) * 512],
                                PT[:, sl2 * 512:(sl2 + 1) * 512],
                                [[1, 512]], ALU.is_ge, 0.0,
                                base=-128 * di, channel_multiplier=-1),
                                waits=[("sc", sc_e)])
                            gate_at[ss] = ("gp", P.cnt["gp"])
                        else:
                            gate_at[ss] = ("sc", sc_e)
                    for j in (3, 2, 1):
                        den_o(nss - j)
                    pe_den = P.cnt["pe"]
                    ve_rr = vop(lambda e, dbank=dbank: e.reciprocal(
                        out=rr[:], in_=dbank[0:1, 0:512]),
                        waits=[("pe", pe_den)] + P.acquire("rr"))

                    def head_tail(dbank=dbank, dnm=dnm, h=h, ve_rr=ve_rr):
                        nonlocal pe_wo_last
                        mm(dbank[:, 0:512], cOnF[0:1, 0:128], rr[:], True, True,
                           inc="pe", waits=[("ve", ve_rr)])
                        pe_bc = P.cnt["pe"]
                        P.release("rr", "pe", pe_bc)
                        ve_rb = vop(lambda e, dbank=dbank: e.tensor_copy(
                            out=recb[:], in_=dbank[:, 0:512]),
                            waits=[("pe", pe_bc)] + P.acquire("recb"))
                        P.release(dnm, "ve", ve_rb)
                        ve_os = vop(lambda e, h=h: e.tensor_tensor(
                            out=osl[:, h * 512:(h + 1) * 512], in0=b2[:, 0:512],
                            in1=recb[:], op=ALU.mult),
                            waits=[("pe", pe_wo_last), ("stPK", stPK_now)])
                        P.release("b2", "ve", ve_os)
                        P.release("recb", "ve", ve_os)

                    tail_ops = [head_tail]
                    if h == NH - 1:
                        for f in tail_ops:
                            f()
                        tail_ops = []
                    if h % 2 == 1 and pending:
                        flush_pending(pending[:1])
                        pending = pending[1:]
                ve_oslab = P.cnt["ve"]

                for tk4 in range(4):
                    tk = t4 * 4 + tk4
                    for h in range(8):
                        w = ([("ve", ve_oslab)] +
                             P.acquire("yA0") + P.acquire("yA1")) if h == 0 else []
                        w = list(w) + [("sc", wo_done[h])]
                        mm(yA[:, 0:512], osl[:, h * 512 + tk4 * 128: h * 512 + tk4 * 128 + 128],
                           ASH[:, h * 1024: h * 1024 + 512], h == 0, h == 7, waits=w)
                        mm(yA[:, 512:1024], osl[:, h * 512 + tk4 * 128: h * 512 + tk4 * 128 + 128],
                           ASH[:, h * 1024 + 512: h * 1024 + 1024], h == 0, h == 7,
                           inc="pe" if h == 7 else None)
                    pe_wo = P.cnt["pe"]
                    pe_wo_last = pe_wo
                    ve_r1 = vop(lambda e, tk=tk: e.tensor_tensor(
                        out=AXs[:, tk * 1024:(tk + 1) * 1024], in0=yA[:],
                        in1=AXs[:, tk * 1024:(tk + 1) * 1024], op=ALU.add),
                        waits=[("pe", pe_wo)])
                    P.release("yA0", "ve", ve_r1)
                    P.release("yA1", "ve", ve_r1)
                    ln_norm(tk)
                    ln_done[tk] = P.cnt["ve"]
                    pending.append(tk)
            P.release("oslw", "pe", P.cnt["pe"])

            flush_pending(pending[:2])
            pend_rest = pending[2:]
            pending = []
            pe_ph2_prev = P.cnt["pe"]

            def load12(e_, qd, hcp):
                sl = w12_idx[0] % 2
                base = 8192 + sl * 2048
                w = P.acquire("w12s%d" % sl)
                last = 0
                for t2 in range(2):
                    hc = (0 if t2 == 0 else 16) + qd * 4 + hcp
                    last = wchunk(ASH[:, base + t2 * 1024: base + (t2 + 1) * 1024],
                                  w12t_h[e_, hc], waits=w if t2 == 0 else ())
                w12_idx[0] += 1
                return (sl, base, last)

            for e_ in range(ne):
                marks["s%d_moe_e%d" % (s, e_)] = P.t
                for qd in range(4):
                    h12_w12 = {}
                    for hcp in range(2):
                        h12_w12[hcp] = load12(e_, qd, hcp)
                    w3w = (P.acquire("w3s") +
                           ([("pe", pe_ph2_prev)] if (e_ == 0 and qd == 0) else []))
                    stW3_now = ("sc", 0)
                    for c4 in range(4):
                        stW3_now = ("sc", wchunk(
                            AV[:, W3B + c4 * 1024: W3B + (c4 + 1) * 1024],
                            w3t_h[e_, qd * 4 + c4],
                            waits=w3w if c4 == 0 else ()))
                    i12 = 0
                    for hcp in range(4):
                        slw, base, tgt12 = h12_w12[hcp]
                        for ts in range(T4):
                            pair = i12 % 2
                            h1p, h2p = ((b2, b3) if pair == 0 else
                                        (yA[:, 0:512], yA[:, 512:1024]))
                            pn = ("b2", "b3") if pair == 0 else ("yA0", "yA1")
                            for dc in range(8):
                                w = ([("sc", tgt12)] + P.acquire(pn[0])) if dc == 0 else ()
                                mm(h1p[:, 0:512] if pair == 0 else h1p,
                                   ASH[:, base + dc * 128: base + dc * 128 + 128],
                                   AFM[:, dc * T + ts * 512:(dc * T + (ts + 1) * 512)],
                                   dc == 0, dc == 7, inc="pe" if dc == 7 else None,
                                   waits=w)
                            pe_h1 = P.cnt["pe"]
                            for dc in range(8):
                                w = P.acquire(pn[1]) if dc == 0 else ()
                                mm(h2p[:, 0:512] if pair == 0 else h2p,
                                   ASH[:, base + 1024 + dc * 128: base + 1024 + dc * 128 + 128],
                                   AFM[:, dc * T + ts * 512:(dc * T + (ts + 1) * 512)],
                                   dc == 0, dc == 7, inc="pe" if dc == 7 else None,
                                   waits=w)
                            pe_h2 = P.cnt["pe"]
                            ssl = i12 % 2
                            sc_si = sop(lambda e, h1p=h1p, pair=pair, ssl=ssl: e.activation(
                                out=s1b[:, ssl * 512:(ssl + 1) * 512],
                                in_=h1p[:, 0:512] if pair == 0 else h1p, func=AF.Silu),
                                waits=[("pe", pe_h1)] + P.acquire("s1b%d" % ssl))
                            P.release(pn[0], "sc", sc_si)
                            ve_hd = vop(lambda e, h2p=h2p, pair=pair, ssl=ssl, hcp=hcp, ts=ts:
                                        e.tensor_tensor(
                                out=AV[:, hcp * T + ts * 512: hcp * T + (ts + 1) * 512],
                                in0=h2p[:, 0:512] if pair == 0 else h2p,
                                in1=s1b[:, ssl * 512:(ssl + 1) * 512], op=ALU.mult),
                                waits=[("pe", pe_h2), ("sc", sc_si),
                                       ("pe", pe_ph2_prev)])
                            P.release(pn[1], "ve", ve_hd)
                            P.release("s1b%d" % ssl, "ve", ve_hd)
                            if ts == T4 - 1:
                                P.release("w12s%d" % slw, "ve", ve_hd)
                            i12 += 1
                            if pend_rest:
                                flush_pending(pend_rest[:1])
                                pend_rest = pend_rest[1:]
                        if hcp + 2 < 4:
                            h12_w12[hcp + 2] = load12(e_, qd, hcp + 2)
                    ve_hidden = P.cnt["ve"]
                    last_q = e_ == ne - 1 and qd == 3
                    for tk in range(TT):
                        yb = tk % 2
                        y0p, y1p = ((yA[:, 0:512], yA[:, 512:1024]) if yb == 0
                                    else (b0, b1))
                        ynm = ("yA0", "yA1") if yb == 0 else ("b0", "b1")
                        for hcp in range(4):
                            w = ([("ve", ve_hidden), stW3_now] +
                                 P.acquire(ynm[0]) + P.acquire(ynm[1])) if hcp == 0 else ()
                            mm(y0p[:, 0:512] if yb else y0p,
                               AV[:, hcp * T + tk * 128: hcp * T + tk * 128 + 128],
                               AV[:, W3B + hcp * 1024: W3B + hcp * 1024 + 512],
                               hcp == 0, hcp == 3, waits=w)
                            mm(y1p[:, 0:512] if yb else y1p,
                               AV[:, hcp * T + tk * 128: hcp * T + tk * 128 + 128],
                               AV[:, W3B + hcp * 1024 + 512: W3B + hcp * 1024 + 1024],
                               hcp == 0, hcp == 3, inc="pe" if hcp == 3 else None)
                        pe_y = P.cnt["pe"]
                        if yb == 0:
                            ve_y = vop(lambda e, tk=tk, e_=e_: e.scalar_tensor_tensor(
                                out=AXs[:, tk * 1024:(tk + 1) * 1024], in0=yA[:],
                                scalar=w_all[:, tk * 8 + e_: tk * 8 + e_ + 1],
                                in1=AXs[:, tk * 1024:(tk + 1) * 1024],
                                op0=ALU.mult, op1=ALU.add), waits=[("pe", pe_y)])
                        else:
                            vop(lambda e, tk=tk, e_=e_: e.scalar_tensor_tensor(
                                out=AXs[:, tk * 1024: tk * 1024 + 512], in0=b0[:, 0:512],
                                scalar=w_all[:, tk * 8 + e_: tk * 8 + e_ + 1],
                                in1=AXs[:, tk * 1024: tk * 1024 + 512],
                                op0=ALU.mult, op1=ALU.add), waits=[("pe", pe_y)])
                            ve_y = vop(lambda e, tk=tk, e_=e_: e.scalar_tensor_tensor(
                                out=AXs[:, tk * 1024 + 512:(tk + 1) * 1024], in0=b1[:, 0:512],
                                scalar=w_all[:, tk * 8 + e_: tk * 8 + e_ + 1],
                                in1=AXs[:, tk * 1024 + 512:(tk + 1) * 1024],
                                op0=ALU.mult, op1=ALU.add))
                        P.release(ynm[0], "ve", ve_y)
                        P.release(ynm[1], "ve", ve_y)
                        if last_q:
                            ln_norm(tk)
                            ve_n = P.cnt["ve"]
                            dma("gpsimd",
                                y_h[s].rearrange("(tt p) c -> p tt c", p=128)[:, tk:tk + 1, :],
                                AXs[:, tk * 1024:(tk + 1) * 1024].rearrange(
                                    "p (a c) -> p a c", a=1),
                                "stO%d" % (tk // 4), waits=[("ve", ve_n)])
                    pe_ph2_prev = P.cnt["pe"]
                    P.release("w3s", "pe", pe_ph2_prev)
            P.release("qkpack", "pe", P.cnt["pe"])
            P.release("ASHvo", "pe", P.cnt["pe"])
            P.release("oslw", "pe", P.cnt["pe"])
            P.release("AFMw", "pe", P.cnt["pe"])

        for g in range(4):
            P.t += 1
            P.q["gpsimd"].append(
                (P.t, lambda e, h=P.sems["stO%d" % g], v=P.cnt["stO%d" % g]:
                 e.wait_ge(h, v)))

        import os as _os
        lim = int(_os.environ.get("KERN_TRUNC", "0")) or None
        for ename in ENGS:
            ops = P.q[ename]

            def runner(eng, ops=ops, lim=lim):
                for t, f in ops:
                    if lim is not None and t > lim:
                        break
                    f(eng)

            getattr(blk, ename)(runner)
        nc._total_emission = P.t
        nc._marks = marks
    return nc


def make_inputs(x, wq, wk, wv, wo, scale, w12, w3, gate, nseq_core, ne=8):
    emat, hmat, ident, ones = host_consts()
    w12t = np.ascontiguousarray(
        w12.reshape(ne, 32, 128, 8, 128).transpose(0, 1, 4, 3, 2).reshape(ne, 32, 128, 1024))
    w3t = np.ascontiguousarray(
        w3.reshape(ne, 1024, 16, 128).transpose(0, 2, 3, 1).reshape(ne, 16, 128, 1024))
    wvt = np.ascontiguousarray(wv.T.reshape(8, 128, 1024))
    wot = np.ascontiguousarray(wo.T.reshape(8, 128, 1024))
    gatet = np.ascontiguousarray(gate.T.reshape(8, 128, 8))
    base = dict(wq=wq, wk=wk, wvt=wvt, wot=wot, scale=scale.reshape(1, 1),
                w12t=w12t, w3t=w3t, gatet=gatet,
                cEq=emat, cEk=emat, cHm=hmat, cId=ident, cOn=ones)
    maps = []
    for c in range(NCORES):
        m = dict(base)
        m["x"] = np.ascontiguousarray(x[c * nseq_core:(c + 1) * nseq_core])
        maps.append(m)
    return maps


def kernel(L0_attn_x, L0_attn_Wq, L0_attn_Wk, L0_attn_Wv, L0_attn_Wo, L0_attn_scale,
           L0_ffn_w12, L0_ffn_w3, L0_ffn_gate_w):
    f = lambda a: np.ascontiguousarray(np.asarray(a, dtype=np.float32))
    x = f(L0_attn_x)
    nseq_core = x.shape[0] // NCORES
    maps = make_inputs(x, f(L0_attn_Wq), f(L0_attn_Wk), f(L0_attn_Wv), f(L0_attn_Wo),
                       f(L0_attn_scale), f(L0_ffn_w12), f(L0_ffn_w3), f(L0_ffn_gate_w),
                       nseq_core)
    nc = build_program(nseq=nseq_core, T=x.shape[1])
    res = run_bass_kernel_spmd(nc, maps, core_ids=list(range(NCORES)))
    return np.concatenate([r["y"] for r in res.results], axis=0)


# version 09
# speedup vs baseline: 1.0994x; 1.0126x over previous
import itertools
import contextlib
import numpy as np

import concourse.bass as bass
import concourse.mybir as mybir
from concourse.bass_utils import run_bass_kernel_spmd

F32 = mybir.dt.float32
BF16 = mybir.dt.bfloat16
AF = mybir.ActivationFunctionType
ALU = mybir.AluOpType
AXL = mybir.AxisListType

NCORES = 8
D = 1024
NH = 8
NEG = -1.0e30

ENGS = ("sync", "tensor", "vector", "scalar", "gpsimd")
SELFCHAIN = {"vector": "ve", "scalar": "sc", "gpsimd": "gp"}


class Prog:

    def __init__(self, nc, stack):
        self.nc = nc
        self.stack = stack
        self.sems = {}
        self.cnt = {}
        self.q = {e: [] for e in ENGS}
        self.seen = {e: {} for e in ENGS}
        self.res = {}
        self.t = 0

    def new_sem(self, name):
        self.sems[name] = self.stack.enter_context(self.nc.semaphore(name))
        self.cnt[name] = 0
        return name

    def wait(self, eng, sname, val):
        if val is None or val <= 0 or self.seen[eng].get(sname, -1) >= val:
            return
        self.seen[eng][sname] = val
        h = self.sems[sname]
        self.t += 1
        self.q[eng].append((self.t, lambda e, h=h, v=val: e.wait_ge(h, v)))

    def op(self, eng, fn, inc=None, inc_by=1, waits=()):
        own = SELFCHAIN.get(eng)
        if own is not None:
            self.wait(eng, own, self.cnt[own])
        for w in waits:
            if w is None:
                continue
            self.wait(eng, w[0], w[1])
        if own is not None and inc is None:
            inc = own
        if inc is not None:
            h = self.sems[inc]
            self.cnt[inc] += inc_by

            def run(e, fn=fn, h=h, b=inc_by):
                fn(e).then_inc(h, b)

            self.t += 1
            self.q[eng].append((self.t, run))
            return self.cnt[inc]
        self.t += 1
        self.q[eng].append((self.t, fn))
        return None

    def acquire(self, name):
        w = self.res.get(name, [])
        self.res[name] = []
        return w

    def release(self, name, sem, cntv):
        self.res.setdefault(name, []).append((sem, cntv))


def host_consts():
    signs = [-1.0, 1.0]
    tri = np.array(list(itertools.product(signs, repeat=3)), dtype=np.float32)
    tri = tri / np.linalg.norm(tri, axis=1, keepdims=True)
    dirs = tri[:NH]
    emat = np.zeros((24, 32), np.float32)
    for h in range(8):
        for i in range(3):
            emat[3 * h + i, 4 * h + i] = 1.0
            emat[3 * h + i, 4 * h + 3] = dirs[h, i]
    hmat = np.zeros((32, 8), np.float32)
    for h in range(8):
        hmat[4 * h:4 * h + 4, h] = 1.0
    ident = np.eye(128, dtype=np.float32)
    ones = np.ones((128, 128), np.float32)
    return emat, hmat, ident, ones


def build_program(nseq=4, T=2048, ne=8):
    TT = T // 128
    T4 = T // 512
    AVW = TT * 1024
    W3B = 4 * T
    nc = bass.Bass("TRN2", target_bir_lowering=False)

    x_h = nc.dram_tensor("x", [nseq, T, D], F32, kind="ExternalInput").ap()
    y_h = nc.dram_tensor("y", [nseq, T, D], F32, kind="ExternalOutput").ap()
    wq_h = nc.dram_tensor("wq", [24, D], F32, kind="ExternalInput").ap()
    wk_h = nc.dram_tensor("wk", [24, D], F32, kind="ExternalInput").ap()
    wvt_h = nc.dram_tensor("wvt", [8, 128, D], F32, kind="ExternalInput").ap()
    wot_h = nc.dram_tensor("wot", [8, 128, D], F32, kind="ExternalInput").ap()
    sc_h = nc.dram_tensor("scale", [1, 1], F32, kind="ExternalInput").ap()
    w12t_h = nc.dram_tensor("w12t", [ne, 32, 128, D], F32, kind="ExternalInput").ap()
    w3t_h = nc.dram_tensor("w3t", [ne, 16, 128, D], F32, kind="ExternalInput").ap()
    gat_h = nc.dram_tensor("gatet", [8, 128, 8], F32, kind="ExternalInput").ap()
    cEq_h = nc.dram_tensor("cEq", [24, 32], F32, kind="ExternalInput").ap()
    cEk_h = nc.dram_tensor("cEk", [24, 32], F32, kind="ExternalInput").ap()
    cHm_h = nc.dram_tensor("cHm", [32, 8], F32, kind="ExternalInput").ap()
    cId_h = nc.dram_tensor("cId", [128, 128], F32, kind="ExternalInput").ap()
    cOn_h = nc.dram_tensor("cOn", [128, 128], F32, kind="ExternalInput").ap()

    with contextlib.ExitStack() as st:
        E = st.enter_context
        cId = E(nc.sbuf_tensor("sb_cId", [128, 128], BF16))
        cOnB = E(nc.sbuf_tensor("cOnB", [128, 128], BF16))
        cOnF = E(nc.sbuf_tensor("cOnF", [128, 128], F32))
        cEq = E(nc.sbuf_tensor("sb_cEq", [24, 32], F32))
        cEk = E(nc.sbuf_tensor("sb_cEk", [24, 32], F32))
        cHm = E(nc.sbuf_tensor("sb_cHm", [32, 8], F32))
        wq4t = E(nc.sbuf_tensor("wq4t", [128, 256], BF16))
        wk4t = E(nc.sbuf_tensor("wk4t", [128, 256], BF16))
        gatet = E(nc.sbuf_tensor("sb_gatet", [128, 64], BF16))
        AXs = E(nc.sbuf_tensor("AXs", [128, TT * 1024], F32))
        AFM = E(nc.sbuf_tensor("AFM", [128, 8 * T], BF16))
        AV = E(nc.sbuf_tensor("AV", [128, AVW], BF16))
        ASH = E(nc.sbuf_tensor("ASH", [128, 12288], BF16))
        osl = E(nc.sbuf_tensor("osl", [128, 4096], BF16))
        PT = E(nc.sbuf_tensor("PT", [128, 2048], BF16))
        qA = E(nc.sbuf_tensor("qA", [128, T], BF16))
        qB = E(nc.sbuf_tensor("qB", [128, T], BF16))
        kA = E(nc.sbuf_tensor("kA", [128, T], BF16))
        kB = E(nc.sbuf_tensor("kB", [128, T], BF16))
        sm = E(nc.sbuf_tensor("sm", [128, 64], F32))
        w_all = E(nc.sbuf_tensor("w_all", [128, TT * 8], F32))
        xc = E(nc.sbuf_tensor("xc", [128, 1024], BF16))
        s1b = E(nc.sbuf_tensor("s1b", [128, 1024], BF16))
        qsq = E(nc.sbuf_tensor("qsq", [32, 1024], F32))
        kn4 = E(nc.sbuf_tensor("kn4", [128, T], BF16))
        rr = E(nc.sbuf_tensor("rr", [1, 512], F32))
        recb = E(nc.sbuf_tensor("recb", [128, 512], F32))
        wstg = E(nc.sbuf_tensor("wstg", [128, 2048], F32))

        b0 = E(nc.psum_tensor("b0", [128, 512], F32))
        b1 = E(nc.psum_tensor("b1", [128, 512], F32))
        b2 = E(nc.psum_tensor("b2", [128, 512], F32))
        b3 = E(nc.psum_tensor("b3", [128, 512], F32))
        yA = E(nc.psum_tensor("yAps", [128, 1024], F32))
        b6 = E(nc.psum_tensor("b6", [128, 512], F32))
        tp = E(nc.psum_tensor("tp", [128, 1024], BF16))

        blk = E(nc.Block())
        P = Prog(nc, st)
        for snm in ("pe", "ve", "sc", "gp", "stC", "stC2", "stCW", "stPK",
                    "stPIa", "stPIb", "stX0", "stX1", "stX2", "stX3",
                    "stO0", "stO1", "stO2", "stO3"):
            P.new_sem(snm)

        def mm(out, lhsT, rhs, start, stop, tile_position=None, inc=None, waits=()):
            return P.op("tensor",
                        lambda e, o=out, l=lhsT, r=rhs, a=start, b=stop, t=tile_position:
                        e.matmul(out=o, lhsT=l, rhs=r, start=a, stop=b, tile_position=t),
                        inc=inc, waits=waits)

        def dma(qeng, out, in_, inc, waits=()):
            return P.op(qeng, lambda e, o=out, i=in_: e.dma_start(out=o, in_=i),
                        inc=inc, inc_by=16, waits=waits)

        def vop(fn, waits=()):
            P.op("vector", fn, waits=waits)
            return P.cnt["ve"]

        def sop(fn, waits=()):
            P.op("scalar", fn, waits=waits)
            return P.cnt["sc"]

        wstate = {"i": 0, "last": [("pe", 0), ("pe", 0)]}

        def wchunk(dst, src, waits=(), eng="scalar"):
            sl = wstate["i"] % 2
            wstate["i"] += 1
            psem = ("stPIa", "stPIb")[sl]
            dma("sync", wstg[:, sl * 1024:(sl + 1) * 1024], src, psem,
                waits=[wstate["last"][sl]])
            if eng == "scalar":
                P.op("scalar",
                     lambda e, d=dst, sl=sl: e.copy(
                         out=d, in_=wstg[:, sl * 1024:(sl + 1) * 1024]),
                     waits=[(psem, P.cnt[psem])] + list(waits))
                wstate["last"][sl] = ("sc", P.cnt["sc"])
                return P.cnt["sc"]
            P.op("vector",
                 lambda e, d=dst, sl=sl: e.tensor_copy(
                     out=d, in_=wstg[:, sl * 1024:(sl + 1) * 1024]),
                 waits=[(psem, P.cnt[psem])] + list(waits))
            wstate["last"][sl] = ("ve", P.cnt["ve"])
            return P.cnt["ve"]

        def ln_norm(tk):
            sl = AXs[:, tk * 1024:(tk + 1) * 1024]
            vop(lambda e: e.bn_stats(out=sm[:, 48:54], in_=sl[:, 0:512]))
            vop(lambda e: e.bn_stats(out=sm[:, 54:60], in_=sl[:, 512:1024]))
            vop(lambda e: e.bn_aggr(out=sm[:, 60:62], in_=sm[:, 48:60]))
            sop(lambda e: e.activation(out=sm[:, 5:6], in_=sm[:, 61:62],
                                       func=AF.Sqrt, bias=sm[:, 13:14]),
                waits=[("ve", P.cnt["ve"])])
            vop(lambda e: e.reciprocal(out=sm[:, 6:7], in_=sm[:, 5:6]),
                waits=[("sc", P.cnt["sc"])])
            vop(lambda e: e.tensor_scalar(out=sm[:, 7:8], in0=sm[:, 60:61],
                                          scalar1=sm[:, 6:7], scalar2=-1.0,
                                          op0=ALU.mult, op1=ALU.mult))
            sop(lambda e: e.activation(out=sl, in_=sl, func=AF.Identity,
                                       scale=sm[:, 6:7], bias=sm[:, 7:8]),
                waits=[("ve", P.cnt["ve"])])
            return ("sc", P.cnt["sc"])

        def gate_chain(tk, gps, pe_gt):
            wsl = w_all[:, tk * 8:(tk + 1) * 8]
            vop(lambda e, gps=gps: e.tensor_copy(out=sm[:, 16:24], in_=gps[:, 0:8]),
                waits=[("pe", pe_gt)])
            vop(lambda e: e.reduce_max(out=sm[:, 8:9], in_=sm[:, 16:24], axis=AXL.X))
            vop(lambda e, wsl=wsl: e.tensor_scalar(out=wsl, in0=sm[:, 16:24],
                                                   scalar1=sm[:, 8:9],
                                                   scalar2=None, op0=ALU.is_equal))
            vop(lambda e, wsl=wsl: e.scalar_tensor_tensor(
                out=sm[:, 24:32], in0=wsl, scalar=NEG, in1=sm[:, 16:24],
                op0=ALU.mult, op1=ALU.add))
            vop(lambda e: e.reduce_max(out=sm[:, 9:10], in_=sm[:, 24:32], axis=AXL.X))
            vop(lambda e: e.tensor_tensor(out=sm[:, 10:11], in0=sm[:, 8:9],
                                          in1=sm[:, 9:10], op=ALU.subtract))
            sop(lambda e: e.activation(out=sm[:, 11:12], in_=sm[:, 10:11],
                                       func=AF.Sigmoid),
                waits=[("ve", P.cnt["ve"])])
            vop(lambda e: e.tensor_scalar(out=sm[:, 14:15], in0=sm[:, 11:12],
                                          scalar1=-1.0, scalar2=1.0, op0=ALU.mult,
                                          op1=ALU.add),
                waits=[("sc", P.cnt["sc"])])
            vop(lambda e: e.tensor_scalar(out=sm[:, 40:48], in0=sm[:, 24:32],
                                          scalar1=sm[:, 9:10], scalar2=None,
                                          op0=ALU.is_equal))
            vop(lambda e, wsl=wsl: e.tensor_scalar(out=wsl, in0=wsl,
                                                   scalar1=sm[:, 11:12],
                                                   scalar2=None, op0=ALU.mult))
            return vop(lambda e, wsl=wsl: e.scalar_tensor_tensor(
                out=wsl, in0=sm[:, 40:48], scalar=sm[:, 14:15], in1=wsl,
                op0=ALU.mult, op1=ALU.add))

        dma("sync", cOnF[:], cOn_h[:], "stC")
        dma("sync", cEq[:], cEq_h[:], "stC")
        dma("sync", cEk[:], cEk_h[:], "stC")
        dma("sync", cHm[:], cHm_h[:], "stC")
        dma("sync", sm[0:1, 62:63], sc_h[:], "stC")
        dma("sync", wstg[0:24, 0:1024], wq_h[:], "stC")
        stC_all = P.cnt["stC"]
        dma("sync", recb[:, 0:128], cId_h[:], "stC2")
        vop(lambda e: e.tensor_copy(out=cId[:], in_=recb[:, 0:128]),
            waits=[("stC2", P.cnt["stC2"])])
        dma("sync", recb[:, 0:64].rearrange("p (j e) -> p j e", e=8),
            gat_h.rearrange("j p e -> p j e"), "stC2",
            waits=[("ve", P.cnt["ve"])])
        vop(lambda e: e.tensor_copy(out=gatet[:], in_=recb[:, 0:64]),
            waits=[("stC2", P.cnt["stC2"])])
        vop(lambda e: e.tensor_copy(out=cOnB[:], in_=cOnF[:]), waits=[("stC", stC_all)])
        vop(lambda e: e.memset(sm[:, 13:14], 1e-5))
        P.op("gpsimd", lambda e: e.memset(kA[:], -1.0))
        P.op("gpsimd", lambda e: e.memset(kB[:], -1.0))
        gp_init = P.cnt["gp"]
        ve_consts = P.cnt["ve"]

        mm(b6[:, 0:1], cOnF[0:1, 0:128], sm[0:1, 62:63], True, True, inc="pe",
           waits=[("stC", stC_all), ("ve", ve_consts)])
        vop(lambda e: e.tensor_copy(out=sm[:, 12:13], in_=b6[:, 0:1]),
            waits=[("pe", P.cnt["pe"])])

        for dc in range(8):
            mm(b6[0:128, 0:32], wstg[0:24, dc * 128:(dc + 1) * 128], cEq[:], True, True,
               inc="pe", waits=[("ve", P.cnt["ve"])])
            vop(lambda e, dc=dc: e.tensor_copy(out=wq4t[:, dc * 32:(dc + 1) * 32],
                                               in_=b6[0:128, 0:32]),
                waits=[("pe", P.cnt["pe"])])
        dma("sync", wstg[0:24, 0:1024], wk_h[:], "stCW", waits=[("pe", P.cnt["pe"])])
        stC_wk = P.cnt["stCW"]
        for dc in range(8):
            mm(b6[0:128, 0:32], wstg[0:24, dc * 128:(dc + 1) * 128], cEk[:], True, True,
               inc="pe", waits=[("ve", P.cnt["ve"]), ("stCW", stC_wk)])
            vop(lambda e, dc=dc: e.tensor_copy(out=wk4t[:, dc * 32:(dc + 1) * 32],
                                               in_=b6[0:128, 0:32]),
                waits=[("pe", P.cnt["pe"])])
            vop(lambda e, dc=dc: e.tensor_scalar(
                out=wk4t[:, dc * 32:(dc + 1) * 32].rearrange("p (h i) -> p h i", i=4)[:, :, 0:3],
                in0=wk4t[:, dc * 32:(dc + 1) * 32].rearrange("p (h i) -> p h i", i=4)[:, :, 0:3],
                scalar1=sm[:, 12:13], scalar2=None, op0=ALU.mult))
        ve_wk = P.cnt["ve"]
        pe_qkbuild = P.cnt["pe"]
        wstate["last"] = [("pe", pe_qkbuild), ("pe", pe_qkbuild)]

        pe_wo_last = 0
        pe_qk_done = 0
        w12_idx = [0]
        marks = {"consts": P.t}

        for s in range(nseq):
            marks["s%d_start" % s] = P.t
            for g in range(4):
                dma("gpsimd",
                    AXs[:].rearrange("p (tt c) -> p tt c", c=1024)[:, g * 4:(g + 1) * 4, :],
                    x_h[s].rearrange("(tt p) c -> p tt c", p=128)[:, g * 4:(g + 1) * 4, :],
                    "stX%d" % g, waits=[("stO%d" % g, 64 * s)])
            aw = P.acquire("ASHvo")
            wv_done = []
            for j in range(8):
                wv_done.append(wchunk(ASH[:, j * 1024:(j + 1) * 1024], wvt_h[j],
                                      waits=aw if j == 0 else ()))

            afm_ve = {}
            osl_guard = P.acquire("oslw")
            qk_first = P.acquire("qkpack")
            vv_pend = None

            def emit_vv():
                nonlocal vv_pend
                if vv_pend is None:
                    return
                tkp, pe_vp = vv_pend
                vv_pend = None
                ve_v = vop(lambda e, tk=tkp: e.tensor_copy(
                    out=AV[:, tk * 1024:(tk + 1) * 1024], in_=yA[:]),
                    waits=[("pe", pe_vp)] + P.acquire("AVw"))
                P.release("yA0", "ve", ve_v)
                P.release("yA1", "ve", ve_v)

            for tk in range(TT):
                ve_cast = vop(lambda e, tk=tk: e.tensor_copy(
                    out=xc[:], in_=AXs[:, tk * 1024:(tk + 1) * 1024]),
                    waits=[("stX%d" % (tk // 4), 16 * (s + 1))] + P.acquire("xc"))
                for j in range(8):
                    w = ([("ve", ve_cast), ("stC", stC_all)] + P.acquire("tp")) if j == 0 else ()
                    P.op("tensor", lambda e, j=j: e.transpose(
                        out=tp[:, j * 128:(j + 1) * 128], in_=xc[:, j * 128:(j + 1) * 128],
                        identity=cId[:]), inc="pe" if j == 7 else None, waits=w)
                pe_tp = P.cnt["pe"]
                P.release("xc", "pe", pe_tp)
                ve_fm = vop(lambda e, tk=tk: e.tensor_copy(
                    out=AFM[:].rearrange("p (j t) -> p j t", t=T)[:, :, tk * 128:(tk + 1) * 128],
                    in_=tp[:].rearrange("p (j t) -> p j t", t=128)),
                    waits=[("pe", pe_tp)] + P.acquire("AFMw"))
                afm_ve[tk] = ve_fm
                P.release("tp", "ve", ve_fm)
                emit_vv()
                for j in range(8):
                    w = ([("ve", ve_fm)] +
                         P.acquire("yA0") + P.acquire("yA1")) if j == 0 else []
                    w = list(w) + [("sc", wv_done[j])]
                    mm(yA[:, 0:512], AFM[:, j * T + tk * 128: j * T + tk * 128 + 128],
                       ASH[:, j * 1024: j * 1024 + 512], j == 0, j == 7, waits=w)
                    mm(yA[:, 512:1024], AFM[:, j * T + tk * 128: j * T + tk * 128 + 128],
                       ASH[:, j * 1024 + 512: j * 1024 + 1024], j == 0, j == 7,
                       inc="pe" if j == 7 else None)
                pe_v = P.cnt["pe"]
                P.release("ASHvo", "pe", pe_v)
                vv_pend = (tk, pe_v)

                if tk % 4 != 3:
                    continue
                t4 = tk // 4
                for dc in range(8):
                    w = ([("ve", ve_fm)] + P.acquire("b6")) if dc == 0 else ()
                    mm(b6[0:32, 0:512], wq4t[:, dc * 32:(dc + 1) * 32],
                       AFM[:, dc * T + t4 * 512: dc * T + (t4 + 1) * 512],
                       dc == 0, dc == 7, inc="pe" if dc == 7 else None, waits=w)
                pe_q = P.cnt["pe"]
                sc_qsq = sop(lambda e: e.activation(out=qsq[:, 0:512],
                                                    in_=b6[0:32, 0:512], func=AF.Square),
                             waits=[("pe", pe_q)] + P.acquire("qsqA"))
                ve_qn = vop(lambda e, t4=t4: e.tensor_copy(
                    out=osl[0:32, t4 * 512:(t4 + 1) * 512], in_=b6[0:32, 0:512]),
                    waits=[("pe", pe_q), ("sc", sc_qsq)] +
                    (osl_guard if t4 == 0 else []))
                P.release("b6", "sc", sc_qsq)
                P.release("b6", "ve", ve_qn)
                for dc in range(8):
                    w = P.acquire("b2") if dc == 0 else ()
                    mm(b2[0:32, 0:512], wk4t[:, dc * 32:(dc + 1) * 32],
                       AFM[:, dc * T + t4 * 512: dc * T + (t4 + 1) * 512],
                       dc == 0, dc == 7, inc="pe" if dc == 7 else None, waits=w)
                pe_k = P.cnt["pe"]
                mm(b3[0:8, 0:512], cHm[:], qsq[:, 0:512], True, True, inc="pe",
                   waits=[("sc", sc_qsq)] + P.acquire("b3"))
                pe_qn = P.cnt["pe"]
                P.release("qsqA", "pe", pe_qn)
                sc_qn = sop(lambda e, t4=t4: e.activation(
                    out=osl[0:8, 2048 + t4 * 512: 2048 + (t4 + 1) * 512],
                    in_=b3[0:8, 0:512], func=AF.Sqrt),
                    waits=[("pe", pe_qn), ("ve", ve_qn)])
                P.release("b3", "sc", sc_qn)
                sc_ksq = sop(lambda e: e.activation(out=qsq[:, 512:1024],
                                                    in_=b2[0:32, 0:512], func=AF.Square),
                             waits=[("pe", pe_k)] + P.acquire("qsqB"))
                ve_kn = vop(lambda e, t4=t4: e.tensor_copy(
                    out=kn4[0:32, t4 * 512:(t4 + 1) * 512], in_=b2[0:32, 0:512]),
                    waits=[("pe", pe_k), ("sc", sc_ksq)])
                P.release("b2", "sc", sc_ksq)
                P.release("b2", "ve", ve_kn)
                mm(b6[0:8, 0:512], cHm[:], qsq[:, 512:1024], True, True, inc="pe",
                   waits=[("sc", sc_ksq)] + P.acquire("b6"))
                pe_kn = P.cnt["pe"]
                P.release("qsqB", "pe", pe_kn)
                ve_km = vop(lambda e, t4=t4: e.reduce_max(
                    out=sm[0:8, 34 + t4:35 + t4], in_=b6[0:8, 0:512], axis=AXL.X),
                    waits=[("pe", pe_kn)])
                P.release("b6", "ve", ve_km)
                pk_w = qk_first + [("gp", gp_init), ("ve", ve_kn)]
                for (dst, g0, srcb) in ((qA, 0, osl), (qB, 4, osl),
                                        (kA, 8, kn4), (kB, 12, kn4)):
                    for hl in range(4):
                        h = (g0 + hl) % 8

                        def f(e, dst=dst, srcb=srcb, h=h, hl=hl, t4=t4):
                            return e.dma_start(
                                out=dst[32 * hl:32 * hl + 4, t4 * 512:(t4 + 1) * 512],
                                in_=srcb[4 * h:4 * h + 4, t4 * 512:(t4 + 1) * 512])
                        P.op("gpsimd", f, inc="stPK", inc_by=16, waits=pk_w)
                        pk_w = ()
            emit_vv()
            ve_vdone = P.cnt["ve"]
            pe_vdone = P.cnt["pe"]

            vop(lambda e: e.reduce_max(out=sm[0:8, 32:33], in_=sm[0:8, 34:34 + T4],
                                       axis=AXL.X))
            sop(lambda e: e.activation(out=sm[0:8, 33:34], in_=sm[0:8, 32:33],
                                       func=AF.Sqrt), waits=[("ve", P.cnt["ve"])])
            ve_mb = vop(lambda e: e.tensor_scalar(
                out=osl[0:8, 2048:2048 + T], in0=osl[0:8, 2048:2048 + T],
                scalar1=sm[0:8, 33:34], scalar2=1.01, op0=ALU.mult, op1=ALU.mult),
                waits=[("sc", P.cnt["sc"])])
            pk_w = [("ve", ve_mb)]
            for h in range(8):
                dst = qA if h < 4 else qB
                hl = h % 4

                def f(e, dst=dst, h=h, hl=hl):
                    return e.dma_start(out=dst[32 * hl + 4:32 * hl + 5, :],
                                       in_=osl[h:h + 1, 2048:2048 + T])
                P.op("gpsimd", f, inc="stPK", inc_by=16, waits=pk_w)
                pk_w = ()
            stPK_now = P.cnt["stPK"]
            pe_qk_done = P.cnt["pe"]

            aw = [("pe", pe_vdone)] + P.acquire("ASHvo")
            wo_done = []
            for j in range(8):
                wo_done.append(wchunk(ASH[:, j * 1024:(j + 1) * 1024], wot_h[j],
                                      waits=aw if j == 0 else ()))

            sbanks = ((b0, "b0"), (b1, "b1"), (b3, "b3"))
            pending = []

            def flush_pending(pend):
                for tk in pend:
                    ve_cast = vop(lambda e, tk=tk: e.tensor_copy(
                        out=xc[:], in_=AXs[:, tk * 1024:(tk + 1) * 1024]),
                        waits=P.acquire("xc") + [ln_done[tk]])
                    for j in range(8):
                        w = ([("ve", ve_cast), ("pe", pe_qk_done)] +
                             P.acquire("tp")) if j == 0 else ()
                        P.op("tensor", lambda e, j=j: e.transpose(
                            out=tp[:, j * 128:(j + 1) * 128],
                            in_=xc[:, j * 128:(j + 1) * 128],
                            identity=cId[:]), inc="pe" if j == 7 else None, waits=w)
                    pe_tp = P.cnt["pe"]
                    P.release("xc", "pe", pe_tp)
                    ve_fm = vop(lambda e, tk=tk: e.tensor_copy(
                        out=AFM[:].rearrange("p (j t) -> p j t", t=T)[:, :, tk * 128:(tk + 1) * 128],
                        in_=tp[:].rearrange("p (j t) -> p j t", t=128)),
                        waits=[("pe", pe_tp)])
                    P.release("tp", "ve", ve_fm)
                    for dc in range(8):
                        w = ([("ve", ve_fm)] + P.acquire("yA0")) if dc == 0 else ()
                        mm(yA[:, 0:8], AFM[:, dc * T + tk * 128: dc * T + tk * 128 + 128],
                           gatet[:, dc * 8:(dc + 1) * 8], dc == 0, dc == 7,
                           inc="pe" if dc == 7 else None, waits=w)
                    pe_gt = P.cnt["pe"]
                    ve_g = gate_chain(tk, yA, pe_gt)
                    P.release("yA0", "ve", ve_g)

            marks["s%d_pass1" % s] = P.t
            ln_done = {}
            os_done = {}
            for t4 in range(T4):
                marks["s%d_p2t%d" % (s, t4)] = P.t
                tail_ops = []
                for h in range(NH):
                    qT = qA if h < 4 else qB
                    kT = kA if h < 4 else kB
                    hl = h % 4
                    nss = 4 * t4 + 4
                    gate_at = {}
                    dbank, dnm = ((b6, "b6") if h % 2 == 0
                                  else (yA[:, 512:1024], "yA1"))

                    def den_o(ss2):
                        sl2 = ss2 % 4
                        first = ss2 == 0
                        last = ss2 == nss - 1
                        w = [gate_at[ss2]]
                        if first:
                            w += P.acquire(dnm)
                        mm(dbank[0:1, 0:512], cOnB[:, 0:1],
                           PT[:, sl2 * 512:(sl2 + 1) * 512],
                           first, last, inc="pe" if last else None, waits=w)
                        w2 = (P.acquire("b2") + [("ve", ve_vdone)]) if first else ()
                        mm(b2[:, 0:512],
                           AV[:, ss2 * 1024 + h * 128: ss2 * 1024 + h * 128 + 128],
                           PT[:, sl2 * 512:(sl2 + 1) * 512],
                           first, last, inc="pe", waits=w2)
                        P.release("PT%d" % sl2, "pe", P.cnt["pe"])

                    for ss in range(nss):
                        bsl, bnm = sbanks[ss % 3]
                        sl2 = ss % 4
                        di = ss - 4 * t4
                        w = [("stPK", stPK_now), ("gp", gp_init)] + P.acquire(bnm)
                        mm(bsl[:, 0:512], kT[32 * hl:32 * hl + 5, ss * 128:(ss + 1) * 128],
                           qT[32 * hl:32 * hl + 5, t4 * 512:(t4 + 1) * 512],
                           True, True, tile_position=(32 * hl, 0),
                           inc="pe", waits=w)
                        pe_g = P.cnt["pe"]
                        if ss == 1 and tail_ops:
                            for f in tail_ops:
                                f()
                            tail_ops = []
                        if ss >= 3:
                            den_o(ss - 3)
                        sc_e = sop(lambda e, sl2=sl2, bsl=bsl: e.activation(
                            out=PT[:, sl2 * 512:(sl2 + 1) * 512], in_=bsl[:, 0:512],
                            func=AF.Exp),
                            waits=[("pe", pe_g)] + P.acquire("PT%d" % sl2))
                        P.release(bnm, "sc", sc_e)
                        if di >= 0:
                            P.op("gpsimd", lambda e, sl2=sl2, di=di: e.affine_select(
                                PT[:, sl2 * 512:(sl2 + 1) * 512],
                                PT[:, sl2 * 512:(sl2 + 1) * 512],
                                [[1, 512]], ALU.is_ge, 0.0,
                                base=-128 * di, channel_multiplier=-1),
                                waits=[("sc", sc_e)])
                            gate_at[ss] = ("gp", P.cnt["gp"])
                        else:
                            gate_at[ss] = ("sc", sc_e)
                    for j in (3, 2, 1):
                        den_o(nss - j)
                    pe_den = P.cnt["pe"]
                    ve_rr = vop(lambda e, dbank=dbank: e.reciprocal(
                        out=rr[:], in_=dbank[0:1, 0:512]),
                        waits=[("pe", pe_den)] + P.acquire("rr"))

                    def head_tail(dbank=dbank, dnm=dnm, h=h, ve_rr=ve_rr):
                        nonlocal pe_wo_last
                        mm(dbank[:, 0:512], cOnF[0:1, 0:128], rr[:], True, True,
                           inc="pe", waits=[("ve", ve_rr)])
                        pe_bc = P.cnt["pe"]
                        P.release("rr", "pe", pe_bc)
                        ve_rb = vop(lambda e, dbank=dbank: e.tensor_copy(
                            out=recb[:], in_=dbank[:, 0:512]),
                            waits=[("pe", pe_bc)] + P.acquire("recb"))
                        P.release(dnm, "ve", ve_rb)
                        ve_os = vop(lambda e, h=h: e.tensor_tensor(
                            out=osl[:, h * 512:(h + 1) * 512], in0=b2[:, 0:512],
                            in1=recb[:], op=ALU.mult),
                            waits=[("pe", pe_wo_last), ("stPK", stPK_now)])
                        os_done[h] = ve_os
                        P.release("b2", "ve", ve_os)
                        P.release("recb", "ve", ve_os)

                    tail_ops = [head_tail]
                    if h == NH - 1:
                        for f in tail_ops:
                            f()
                        tail_ops = []
                    if h % 2 == 1 and pending:
                        flush_pending(pending[:1])
                        pending = pending[1:]
                ve_oslab = P.cnt["ve"]

                for tk4 in range(4):
                    tk = t4 * 4 + tk4
                    for h in range(8):
                        w = (P.acquire("yA0") + P.acquire("yA1")) if h == 0 else []
                        w = list(w) + [("ve", os_done[h]), ("sc", wo_done[h])]
                        mm(yA[:, 0:512], osl[:, h * 512 + tk4 * 128: h * 512 + tk4 * 128 + 128],
                           ASH[:, h * 1024: h * 1024 + 512], h == 0, h == 7, waits=w)
                        mm(yA[:, 512:1024], osl[:, h * 512 + tk4 * 128: h * 512 + tk4 * 128 + 128],
                           ASH[:, h * 1024 + 512: h * 1024 + 1024], h == 0, h == 7,
                           inc="pe" if h == 7 else None)
                    pe_wo = P.cnt["pe"]
                    pe_wo_last = pe_wo
                    ve_r1 = vop(lambda e, tk=tk: e.tensor_tensor(
                        out=AXs[:, tk * 1024:(tk + 1) * 1024], in0=yA[:],
                        in1=AXs[:, tk * 1024:(tk + 1) * 1024], op=ALU.add),
                        waits=[("pe", pe_wo)])
                    P.release("yA0", "ve", ve_r1)
                    P.release("yA1", "ve", ve_r1)
                    ln_done[tk] = ln_norm(tk)
                    pending.append(tk)
            P.release("oslw", "pe", P.cnt["pe"])

            flush_pending(pending[:2])
            pend_rest = pending[2:]
            pending = []
            pe_ph2_prev = P.cnt["pe"]

            def load12(e_, qd, hcp):
                sl = w12_idx[0] % 2
                base = 8192 + sl * 2048
                w = P.acquire("w12s%d" % sl)
                last = 0
                for t2 in range(2):
                    hc = (0 if t2 == 0 else 16) + qd * 4 + hcp
                    last = wchunk(ASH[:, base + t2 * 1024: base + (t2 + 1) * 1024],
                                  w12t_h[e_, hc], waits=w if t2 == 0 else ())
                w12_idx[0] += 1
                return (sl, base, last)

            for e_ in range(ne):
                marks["s%d_moe_e%d" % (s, e_)] = P.t
                for qd in range(4):
                    h12_w12 = {}
                    for hcp in range(2):
                        h12_w12[hcp] = load12(e_, qd, hcp)
                    w3w = (P.acquire("w3s") +
                           ([("pe", pe_ph2_prev)] if (e_ == 0 and qd == 0) else []))
                    stW3_now = ("sc", 0)
                    for c4 in range(4):
                        stW3_now = ("sc", wchunk(
                            AV[:, W3B + c4 * 1024: W3B + (c4 + 1) * 1024],
                            w3t_h[e_, qd * 4 + c4],
                            waits=w3w if c4 == 0 else ()))
                    i12 = 0
                    for hcp in range(4):
                        slw, base, tgt12 = h12_w12[hcp]
                        for ts in range(T4):
                            pair = i12 % 2
                            h1p, h2p = ((b2, b3) if pair == 0 else
                                        (yA[:, 0:512], yA[:, 512:1024]))
                            pn = ("b2", "b3") if pair == 0 else ("yA0", "yA1")
                            for dc in range(8):
                                w = ([("sc", tgt12)] + P.acquire(pn[0])) if dc == 0 else ()
                                mm(h1p[:, 0:512] if pair == 0 else h1p,
                                   ASH[:, base + dc * 128: base + dc * 128 + 128],
                                   AFM[:, dc * T + ts * 512:(dc * T + (ts + 1) * 512)],
                                   dc == 0, dc == 7, inc="pe" if dc == 7 else None,
                                   waits=w)
                            pe_h1 = P.cnt["pe"]
                            for dc in range(8):
                                w = P.acquire(pn[1]) if dc == 0 else ()
                                mm(h2p[:, 0:512] if pair == 0 else h2p,
                                   ASH[:, base + 1024 + dc * 128: base + 1024 + dc * 128 + 128],
                                   AFM[:, dc * T + ts * 512:(dc * T + (ts + 1) * 512)],
                                   dc == 0, dc == 7, inc="pe" if dc == 7 else None,
                                   waits=w)
                            pe_h2 = P.cnt["pe"]
                            ssl = i12 % 2
                            sc_si = sop(lambda e, h1p=h1p, pair=pair, ssl=ssl: e.activation(
                                out=s1b[:, ssl * 512:(ssl + 1) * 512],
                                in_=h1p[:, 0:512] if pair == 0 else h1p, func=AF.Silu),
                                waits=[("pe", pe_h1)] + P.acquire("s1b%d" % ssl))
                            P.release(pn[0], "sc", sc_si)
                            ve_hd = vop(lambda e, h2p=h2p, pair=pair, ssl=ssl, hcp=hcp, ts=ts:
                                        e.tensor_tensor(
                                out=AV[:, hcp * T + ts * 512: hcp * T + (ts + 1) * 512],
                                in0=h2p[:, 0:512] if pair == 0 else h2p,
                                in1=s1b[:, ssl * 512:(ssl + 1) * 512], op=ALU.mult),
                                waits=[("pe", pe_h2), ("sc", sc_si),
                                       ("pe", pe_ph2_prev)])
                            P.release(pn[1], "ve", ve_hd)
                            P.release("s1b%d" % ssl, "ve", ve_hd)
                            if ts == T4 - 1:
                                P.release("w12s%d" % slw, "ve", ve_hd)
                            i12 += 1
                            if pend_rest:
                                flush_pending(pend_rest[:1])
                                pend_rest = pend_rest[1:]
                        if hcp + 2 < 4:
                            h12_w12[hcp + 2] = load12(e_, qd, hcp + 2)
                    ve_hidden = P.cnt["ve"]
                    last_q = e_ == ne - 1 and qd == 3
                    for tk in range(TT):
                        yb = tk % 2
                        y0p, y1p = ((yA[:, 0:512], yA[:, 512:1024]) if yb == 0
                                    else (b0, b1))
                        ynm = ("yA0", "yA1") if yb == 0 else ("b0", "b1")
                        for hcp in range(4):
                            w = ([("ve", ve_hidden), stW3_now] +
                                 P.acquire(ynm[0]) + P.acquire(ynm[1])) if hcp == 0 else ()
                            mm(y0p[:, 0:512] if yb else y0p,
                               AV[:, hcp * T + tk * 128: hcp * T + tk * 128 + 128],
                               AV[:, W3B + hcp * 1024: W3B + hcp * 1024 + 512],
                               hcp == 0, hcp == 3, waits=w)
                            mm(y1p[:, 0:512] if yb else y1p,
                               AV[:, hcp * T + tk * 128: hcp * T + tk * 128 + 128],
                               AV[:, W3B + hcp * 1024 + 512: W3B + hcp * 1024 + 1024],
                               hcp == 0, hcp == 3, inc="pe" if hcp == 3 else None)
                        pe_y = P.cnt["pe"]
                        if yb == 0:
                            ve_y = vop(lambda e, tk=tk, e_=e_: e.scalar_tensor_tensor(
                                out=AXs[:, tk * 1024:(tk + 1) * 1024], in0=yA[:],
                                scalar=w_all[:, tk * 8 + e_: tk * 8 + e_ + 1],
                                in1=AXs[:, tk * 1024:(tk + 1) * 1024],
                                op0=ALU.mult, op1=ALU.add), waits=[("pe", pe_y)])
                        else:
                            vop(lambda e, tk=tk, e_=e_: e.scalar_tensor_tensor(
                                out=AXs[:, tk * 1024: tk * 1024 + 512], in0=b0[:, 0:512],
                                scalar=w_all[:, tk * 8 + e_: tk * 8 + e_ + 1],
                                in1=AXs[:, tk * 1024: tk * 1024 + 512],
                                op0=ALU.mult, op1=ALU.add), waits=[("pe", pe_y)])
                            ve_y = vop(lambda e, tk=tk, e_=e_: e.scalar_tensor_tensor(
                                out=AXs[:, tk * 1024 + 512:(tk + 1) * 1024], in0=b1[:, 0:512],
                                scalar=w_all[:, tk * 8 + e_: tk * 8 + e_ + 1],
                                in1=AXs[:, tk * 1024 + 512:(tk + 1) * 1024],
                                op0=ALU.mult, op1=ALU.add))
                        P.release(ynm[0], "ve", ve_y)
                        P.release(ynm[1], "ve", ve_y)
                        if last_q:
                            ln_n = ln_norm(tk)
                            dma("gpsimd",
                                y_h[s].rearrange("(tt p) c -> p tt c", p=128)[:, tk:tk + 1, :],
                                AXs[:, tk * 1024:(tk + 1) * 1024].rearrange(
                                    "p (a c) -> p a c", a=1),
                                "stO%d" % (tk // 4), waits=[ln_n])
                    pe_ph2_prev = P.cnt["pe"]
                    P.release("w3s", "pe", pe_ph2_prev)
            P.release("qkpack", "pe", P.cnt["pe"])
            P.release("ASHvo", "pe", P.cnt["pe"])
            P.release("oslw", "pe", P.cnt["pe"])
            P.release("AFMw", "pe", P.cnt["pe"])

        for g in range(4):
            P.t += 1
            P.q["gpsimd"].append(
                (P.t, lambda e, h=P.sems["stO%d" % g], v=P.cnt["stO%d" % g]:
                 e.wait_ge(h, v)))

        import os as _os
        lim = int(_os.environ.get("KERN_TRUNC", "0")) or None
        for ename in ENGS:
            ops = P.q[ename]

            def runner(eng, ops=ops, lim=lim):
                for t, f in ops:
                    if lim is not None and t > lim:
                        break
                    f(eng)

            getattr(blk, ename)(runner)
        nc._total_emission = P.t
        nc._marks = marks
    return nc


def make_inputs(x, wq, wk, wv, wo, scale, w12, w3, gate, nseq_core, ne=8):
    emat, hmat, ident, ones = host_consts()
    w12t = np.ascontiguousarray(
        w12.reshape(ne, 32, 128, 8, 128).transpose(0, 1, 4, 3, 2).reshape(ne, 32, 128, 1024))
    w3t = np.ascontiguousarray(
        w3.reshape(ne, 1024, 16, 128).transpose(0, 2, 3, 1).reshape(ne, 16, 128, 1024))
    wvt = np.ascontiguousarray(wv.T.reshape(8, 128, 1024))
    wot = np.ascontiguousarray(wo.T.reshape(8, 128, 1024))
    gatet = np.ascontiguousarray(gate.T.reshape(8, 128, 8))
    base = dict(wq=wq, wk=wk, wvt=wvt, wot=wot, scale=scale.reshape(1, 1),
                w12t=w12t, w3t=w3t, gatet=gatet,
                cEq=emat, cEk=emat, cHm=hmat, cId=ident, cOn=ones)
    maps = []
    for c in range(NCORES):
        m = dict(base)
        m["x"] = np.ascontiguousarray(x[c * nseq_core:(c + 1) * nseq_core])
        maps.append(m)
    return maps


def kernel(L0_attn_x, L0_attn_Wq, L0_attn_Wk, L0_attn_Wv, L0_attn_Wo, L0_attn_scale,
           L0_ffn_w12, L0_ffn_w3, L0_ffn_gate_w):
    f = lambda a: np.ascontiguousarray(np.asarray(a, dtype=np.float32))
    x = f(L0_attn_x)
    nseq_core = x.shape[0] // NCORES
    maps = make_inputs(x, f(L0_attn_Wq), f(L0_attn_Wk), f(L0_attn_Wv), f(L0_attn_Wo),
                       f(L0_attn_scale), f(L0_ffn_w12), f(L0_ffn_w3), f(L0_ffn_gate_w),
                       nseq_core)
    nc = build_program(nseq=nseq_core, T=x.shape[1])
    res = run_bass_kernel_spmd(nc, maps, core_ids=list(range(NCORES)))
    return np.concatenate([r["y"] for r in res.results], axis=0)
